# version 01
# baseline (speedup 1.0000x reference)
import numpy as np
import ml_dtypes
from contextlib import ExitStack

import concourse.bass as bass
import concourse.mybir as mybir
from concourse.bass_utils import run_bass_kernel_spmd

F32 = mybir.dt.float32
BF16 = mybir.dt.bfloat16
I32 = mybir.dt.int32
AF = mybir.ActivationFunctionType
ALU = mybir.AluOpType
AXX = mybir.AxisListType.X

NCORES = 8
H = 4096
DIN = 8192
NST = 16
R = 256
TLOC = 1024
HALO = 3
TF = TLOC + HALO
MAGIC = float(np.float32(1.5 * 2 ** 23))
LN127 = float(np.log(127.0))
SC24 = float(2.0 ** -24)

TCH_F = [(i * 128, 128) for i in range(8)] + [(1024, 3)]
DBG_NOQUANT = False
TCH_R = [(i * 128, 128) for i in range(8)]
TFREE_F = [(0, 512), (512, 512), (1024, 3)]


class Ctr:
    def __init__(self, h):
        self.h = h
        self.n = 0

    def bump(self, k=1):
        self.n += k
        return self.n


def build_nc(debug=False, stop_after=None):
    nc = bass.Bass("TRN2", target_bir_lowering=False)

    x_in = nc.dram_tensor("x", [TF, H], F32, kind="ExternalInput").ap()
    out_e = nc.dram_tensor("out", [TLOC, H], F32, kind="ExternalOutput").ap()
    w_in_t = nc.dram_tensor("w_in_t", [H, 2 * DIN], F32, kind="ExternalInput").ap()
    w_out_t = nc.dram_tensor("w_out_t", [DIN, H], F32, kind="ExternalInput").ap()
    w_int_t = nc.dram_tensor("w_int_t", [H, H], I32, kind="ExternalInput").ap()
    scale_rep = nc.dram_tensor("scale_rep", [H, H], I32, kind="ExternalInput").ap()
    w_x_t = nc.dram_tensor("w_x_t", [DIN, 288], F32, kind="ExternalInput").ap()
    w_dt_t = nc.dram_tensor("w_dt_t", [R, DIN], F32, kind="ExternalInput").ap()
    convw_in = nc.dram_tensor("convw", [128, 64, 4], F32, kind="ExternalInput").ap()
    convb_in = nc.dram_tensor("convb", [128, 64], F32, kind="ExternalInput").ap()
    bdt_in = nc.dram_tensor("bdt", [128, 64], F32, kind="ExternalInput").ap()
    alog_in = nc.dram_tensor("alog", [128, 64 * NST], F32, kind="ExternalInput").ap()
    dvec_in = nc.dram_tensor("dvec", [128, 64], F32, kind="ExternalInput").ap()
    fbias_in = nc.dram_tensor("fbias", [1, H], F32, kind="ExternalInput").ap()
    mask_in = nc.dram_tensor("mask", [128, 1], F32, kind="ExternalInput").ap()
    oh_in = nc.dram_tensor("onehot", [128, NCORES], F32, kind="ExternalInput").ap()
    identf_in = nc.dram_tensor("identf", [128, 128], F32, kind="ExternalInput").ap()

    wdq_bf = nc.dram_tensor("wdq_bf", [H, H], BF16).ap()
    win_bf = nc.dram_tensor("win_bf", [H, 2 * DIN], BF16).ap()
    wout_bf = nc.dram_tensor("wout_bf", [DIN, H], BF16).ap()
    wx_bf = nc.dram_tensor("wx_bf", [DIN, 288], BF16).ap()
    wdt_bf = nc.dram_tensor("wdt_bf", [R, DIN], BF16).ap()
    u_scr = nc.dram_tensor("u_scr", [DIN, TLOC], F32).ap()
    ubf_scr = nc.dram_tensor("ubf_scr", [DIN, TLOC], BF16).ap()
    gate_scr = nc.dram_tensor("gate_scr", [DIN, TLOC], F32).ap()
    dt_scr = nc.dram_tensor("dt_scr", [DIN, TLOC], F32).ap()
    p_scr = nc.dram_tensor("p_scr", [DIN, TLOC], F32).ap()
    y0_scr = nc.dram_tensor("y0_scr", [DIN, TLOC], F32).ap()
    yf_scr = nc.dram_tensor("yf_scr", [DIN, TLOC], BF16).ap()
    u1_scr = nc.dram_tensor("u1_scr", [TF, H], F32).ap()
    u2_scr = nc.dram_tensor("u2_scr", [TLOC, H], F32).ap()
    brow_d = nc.dram_tensor("brow_d", [NST, TLOC], BF16).ap()
    crow_d = nc.dram_tensor("crow_d", [NST, TLOC], BF16).ap()
    ag_in_t = nc.dram_tensor("ag_in", [2, 128, 1024], F32)
    ag_out_t = nc.dram_tensor("ag_out", [NCORES, 2, 128, 1024], F32)

    pers = ExitStack()

    def sbp(name, shape, dtype):
        return pers.enter_context(nc.sbuf_tensor(name, shape, dtype))

    identf = sbp("sb_identf", [128, 128], F32)
    maskc = sbp("maskc", [128, 1], F32)
    oh = sbp("oh", [128, NCORES], F32)
    acol = sbp("acol", [128, 64 * NST], F32)
    dcolv = sbp("dcolv", [128, 64], F32)
    convwc = sbp("convwc", [128, 64, 4], F32)
    convbc = sbp("convbc", [128, 64], F32)
    bdtc = sbp("bdtc", [128, 64], F32)
    ptotc = sbp("ptotc", [128, 64], F32)
    zall = sbp("zall", [128, 64 * NST], F32)
    dtotT = sbp("dtotT", [128, 64 * NST], F32)
    sinT = sbp("sinT", [128, 64 * NST], F32)
    biasrow = sbp("biasrow", [1, H], BF16)
    fbrow = sbp("fbrow", [1, H], F32)
    ones1b = sbp("ones1b", [1, 128], BF16)
    ones1f = sbp("ones1f", [1, 128], F32)
    onest = sbp("onest", [128, TLOC], F32)
    cst = sbp("cst", [128, 5], F32)
    vdm = sbp("vdm", [1, 2], F32)
    sdm = sbp("sdm", [1, 2], F32)

    blk_cm = nc.Block()
    block = blk_cm.__enter__()

    sem_ld = pers.enter_context(nc.semaphore("sm_ld"))
    sem_st = pers.enter_context(nc.semaphore("sm_st"))
    sem_t = pers.enter_context(nc.semaphore("sm_t"))
    sem_v = pers.enter_context(nc.semaphore("sm_v"))
    sem_s = pers.enter_context(nc.semaphore("sm_s"))
    sem_cc = pers.enter_context(nc.semaphore("sm_cc"))
    sem_bar = pers.enter_context(nc.semaphore("sm_bar"))
    sem_vb = pers.enter_context(nc.semaphore("sm_vb"))
    ring_sems = [pers.enter_context(nc.semaphore(f"sm_r{i}")) for i in range(20)]
    LD, ST, TT, VV, SS, CC, BAR = (Ctr(sem_ld), Ctr(sem_st), Ctr(sem_t),
                                   Ctr(sem_v), Ctr(sem_s), Ctr(sem_cc),
                                   Ctr(sem_bar))
    VB = Ctr(sem_vb)
    RC = [Ctr(s) for s in ring_sems]

    class Ring:

        def __init__(self, base, n):
            self.cs = RC[base:base + n]

        def load(self, e, slot, out, in_):
            e.dma_start(out=out, in_=in_).then_inc(self.cs[slot].h, 16)
            self.cs[slot].bump(16)
            return self.cs[slot].n

        def wait(self, e, slot, val=None):
            e.wait_ge(self.cs[slot].h, self.cs[slot].n if val is None else val)

    def vwait(e):
        e.tensor_copy(vdm[0:1, 0:1], vdm[0:1, 1:2]).then_inc(sem_vb, 1)
        e.wait_ge(sem_vb, VB.bump(1))

    def vinc(e, k=1):
        e.tensor_copy(vdm[0:1, 0:1], vdm[0:1, 1:2]).then_inc(sem_v, k)

    def sinc(e, k=1):
        e.activation(sdm[0:1, 0:1], sdm[0:1, 1:2], AF.Copy).then_inc(sem_s, k)

    def barrier():
        tgt = BAR.n + 5
        BAR.n = tgt
        snap = (LD.n, ST.n, TT.n, VV.n, SS.n)
        for name in ("sync", "vector", "scalar", "tensor", "gpsimd"):
            def f(e, tgt=tgt, name=name, snap=snap,
                  rsnap=[c.n for c in RC]):
                if name == "gpsimd":
                    for sem, v in zip((sem_ld, sem_st, sem_t, sem_v, sem_s), snap):
                        e.wait_ge(sem, v)
                    for c, v in zip(RC, rsnap):
                        if v:
                            e.wait_ge(c.h, v)
                e.sem_inc(sem_bar, 1)
                e.wait_ge(sem_bar, tgt)
            getattr(block, name)(f)

    p0 = ExitStack()
    fin = [p0.enter_context(nc.sbuf_tensor(f"fin{i}", [128, 8192], F32))
           for i in range(2)]
    bout = [p0.enter_context(nc.sbuf_tensor(f"bout{i}", [128, 8192], BF16))
            for i in range(2)]
    wint = [p0.enter_context(nc.sbuf_tensor(f"wint{i}", [128, 2048], I32))
            for i in range(2)]
    wsc = [p0.enter_context(nc.sbuf_tensor(f"wsc{i}", [128, 2048], I32))
           for i in range(2)]
    wfc = [p0.enter_context(nc.sbuf_tensor(f"wfc{i}", [128, 2048], F32))
           for i in range(2)]
    sfc = [p0.enter_context(nc.sbuf_tensor(f"sfc{i}", [128, 2048], F32))
           for i in range(2)]

    jobs = []
    for r0 in range(0, H, 128):
        jobs.append(("f", w_in_t[r0:r0 + 128, 0:8192],
                     win_bf[r0:r0 + 128, 0:8192], 8192))
        jobs.append(("f", w_in_t[r0:r0 + 128, 8192:16384],
                     win_bf[r0:r0 + 128, 8192:16384], 8192))
    for r0 in range(0, DIN, 128):
        jobs.append(("f", w_out_t[r0:r0 + 128, :], wout_bf[r0:r0 + 128, :], 4096))
    for r0 in range(0, DIN, 128):
        jobs.append(("f", w_x_t[r0:r0 + 128, :], wx_bf[r0:r0 + 128, :], 288))
    for r0 in range(0, R, 128):
        jobs.append(("f", w_dt_t[r0:r0 + 128, :], wdt_bf[r0:r0 + 128, :], DIN))
    for r0 in range(0, H, 128):
        for c0 in range(0, H, 2048):
            jobs.append(("q", r0, c0, 2048))
    NJ = len(jobs)

    NSMALL = 9
    ld0, vv0, ss0 = LD.n, VV.n, SS.n
    RF = Ring(0, 2)
    RQ = Ring(2, 2)
    RBS = Ring(4, 2)
    rf_t, rq_t, rbs_t = {}, {}, {}
    cf = [RF.cs[i].n for i in range(2)]
    cq = [RQ.cs[i].n for i in range(2)]
    cbs = [RBS.cs[i].n for i in range(2)]
    for j, jb in enumerate(jobs):
        p = j % 2
        if jb[0] == "f":
            cf[p] += 16
            rf_t[j] = cf[p]
        else:
            cq[p] += 32
            rq_t[j] = cq[p]
        cbs[p] += 16
        rbs_t[j] = cbs[p]
    ss_at = []
    a = ss0 + 1
    for jb in jobs:
        if jb[0] == "q":
            a += 2
        ss_at.append(a)
    SS.n = a
    vdone = [vv0 + 1 + j + 1 for j in range(NJ)]
    VV.n = vv0 + 1 + NJ + 1
    LD.n = ld0 + 16 * NSMALL

    def p0_sync(e):
        e.dma_start(out=identf[:], in_=identf_in).then_inc(sem_ld, 16)
        e.dma_start(out=maskc[:], in_=mask_in).then_inc(sem_ld, 16)
        e.dma_start(out=oh[:], in_=oh_in).then_inc(sem_ld, 16)
        e.dma_start(out=acol[:], in_=alog_in).then_inc(sem_ld, 16)
        e.dma_start(out=dcolv[:], in_=dvec_in).then_inc(sem_ld, 16)
        e.dma_start(out=convwc[:], in_=convw_in).then_inc(sem_ld, 16)
        e.dma_start(out=convbc[:], in_=convb_in).then_inc(sem_ld, 16)
        e.dma_start(out=bdtc[:], in_=bdt_in).then_inc(sem_ld, 16)
        e.dma_start(out=fbrow[:], in_=fbias_in).then_inc(sem_ld, 16)
        for j, jb in enumerate(jobs):
            p = j % 2
            if j >= 2:
                e.wait_ge(sem_v, vdone[j - 2])
            if jb[0] == "f":
                RF.load(e, p, fin[p][:, :jb[3]], jb[1])
            else:
                r0, c0 = jb[1], jb[2]
                RQ.load(e, p, wint[p][:], w_int_t[r0:r0 + 128, c0:c0 + 2048])
                RQ.load(e, p, wsc[p][:], scale_rep[r0:r0 + 128, c0:c0 + 2048])
    block.sync(p0_sync)
    base_small = ld0 + 16 * NSMALL

    def p0_scalar(e):
        e.wait_ge(sem_ld, base_small)
        e.activation(acol[:], acol[:], AF.Exp)
        sinc(e, 1)
        for j, jb in enumerate(jobs):
            p = j % 2
            if jb[0] == "q":
                e.wait_ge(sem_v, vv0 + 1)
                if j >= 2:
                    e.wait_ge(sem_v, vdone[j - 2])
                RQ.wait(e, p, rq_t[j])
                e.activation(wfc[p][:], wint[p][:], AF.Copy)
                e.activation(sfc[p][:], wsc[p][:], AF.Copy,
                             scale=cst[:, 4:5])
                sinc(e, 2)
    block.scalar(p0_scalar)

    def p0_vector(e):
        e.memset(ones1b[:], 1.0)
        e.memset(ones1f[:], 1.0)
        e.memset(onest[:], 1.0)
        e.memset(cst[:, 0:1], 1e-5)
        e.memset(cst[:, 1:2], 1e-6)
        e.memset(cst[:, 2:3], LN127)
        e.memset(cst[:, 3:4], -0.5)
        e.memset(cst[:, 4:5], SC24)
        vinc(e, 1)
        e.wait_ge(sem_s, ss0 + 1)
        e.tensor_scalar_mul(acol[:], acol[:], -1.0)
        for j, jb in enumerate(jobs):
            p = j % 2
            if j >= 2:
                RBS.wait(e, p, rbs_t[j - 2])
            if jb[0] == "f":
                RF.wait(e, p, rf_t[j])
                e.tensor_copy(bout[p][:, :jb[3]], fin[p][:, :jb[3]])
                vinc(e, 1)
            else:
                e.wait_ge(sem_s, ss_at[j])
                e.tensor_mul(wfc[p][:], wfc[p][:], sfc[p][:])
                e.tensor_copy(bout[p][:, :2048], wfc[p][:])
                vinc(e, 1)
        e.wait_ge(sem_ld, base_small)
        e.tensor_copy(biasrow[:], fbrow[:])
        vinc(e, 1)
    block.vector(p0_vector)

    def p0_store(e):
        for j, jb in enumerate(jobs):
            p = j % 2
            e.wait_ge(sem_v, vdone[j])
            if jb[0] == "f":
                RBS.load(e, p, jb[2], bout[p][:, :jb[3]])
            else:
                r0, c0 = jb[1], jb[2]
                RBS.load(e, p, wdq_bf[r0:r0 + 128, c0:c0 + 2048],
                         bout[p][:, :2048])
    block.gpsimd(p0_store)

    barrier()
    p0.close()

    def emit_ln_quant(src, tch, quant, actT, tag, dbg_hist=False):
        dbg_lnt = None
        if dbg_hist:
            dbg_lnt = nc.dram_tensor(f"dbg_lnt_{tag}", [len(tch), 128, H], F32,
                                     kind="ExternalOutput").ap()
        sc = ExitStack()
        xt = [sc.enter_context(nc.sbuf_tensor(f"{tag}xt{i}", [128, H], F32))
              for i in range(2)]
        sqt = sc.enter_context(nc.sbuf_tensor(f"{tag}sq", [128, H], F32))
        lnt = sc.enter_context(nc.sbuf_tensor(f"{tag}ln", [128, H], F32))
        stat = sc.enter_context(nc.sbuf_tensor(f"{tag}st", [128, 20], F32))
        hvar = sc.enter_context(nc.sbuf_tensor(f"{tag}hv", [128, 1], F32))
        hrstd = sc.enter_context(nc.sbuf_tensor(f"{tag}hr", [128, 1], F32))
        hrm = sc.enter_context(nc.sbuf_tensor(f"{tag}hm", [128, 1], F32))
        hinv = sc.enter_context(nc.sbuf_tensor(f"{tag}hi", [128, 1], F32))
        shist = sc.enter_context(nc.sbuf_tensor(f"{tag}sh", [128, len(tch), 24],
                                                F32))
        ptq = [sc.enter_context(nc.psum_tensor(f"{tag}pt{i}", [128, 128], F32))
               for i in range(2)]
        NC = len(tch)
        RX = Ring(6, 2)
        RLD = Ring(8, 1)
        xt_t = []
        _c = [RX.cs[i].n for i in range(2)]
        for ci in range(NC):
            _c[ci % 2] += 16
            xt_t.append(_c[ci % 2])
        for i in range(2):
            RX.cs[i].n = _c[i]
        ld0, vv0, ss0, tt0 = LD.n, VV.n, SS.n, TT.n
        vpc = 3 if quant else 2
        base_s = 3 if quant else 2
        spc = base_s + 32
        VV.n = vv0 + vpc * NC
        SS.n = ss0 + spc * NC
        TT.n = tt0 + 32 * NC

        rld0 = RLD.cs[0].n
        if dbg_lnt is not None:
            RLD.cs[0].n += 16 * len(tch)

        def evac_done(gidx):
            ci, c = divmod(gidx, 32)
            return ss0 + spc * ci + base_s + c + 1

        def s_sync(e):
            for ci, (t0, ts_) in enumerate(tch):
                if ci >= 2:
                    e.wait_ge(sem_v, vv0 + vpc * (ci - 1))
                e.dma_start(out=xt[ci % 2][0:ts_, :], in_=src[t0:t0 + ts_, :]
                            ).then_inc(RX.cs[ci % 2].h, 16)
        block.sync(s_sync)

        def s_scalar(e):
            for ci, (t0, ts_) in enumerate(tch):
                p = ci % 2
                sb = ss0 + spc * ci
                vb = vv0 + vpc * ci
                RX.wait(e, ci % 2, xt_t[ci])
                if ci >= 1:
                    e.wait_ge(sem_v, vv0 + vpc * (ci - 1) + 1)
                e.activation(sqt[0:ts_, :], xt[p][0:ts_, :], AF.Square
                             )
                sinc(e, 1)
                e.wait_ge(sem_v, vb + 1)
                e.activation(stat[0:ts_, 6:7], hvar[0:ts_, :], AF.Ln,
                             bias=cst[0:ts_, 0:1])
                e.activation(hrstd[0:ts_, :], stat[0:ts_, 6:7], AF.Exp,
                             scale=cst[0:ts_, 3:4])
                sinc(e, 1)
                if quant:
                    e.wait_ge(sem_v, vb + 2)
                    e.activation(stat[0:ts_, 12:13], hrm[0:ts_, :], AF.Ln)
                    e.activation(hinv[0:ts_, :], stat[0:ts_, 12:13], AF.Exp,
                                 scale=-1.0, bias=cst[0:ts_, 2:3])
                    sinc(e, 1)
                for c in range(32):
                    e.wait_ge(sem_t, tt0 + 32 * ci + c + 1)
                    e.activation(actT[:, c, t0:t0 + ts_], ptq[c % 2][:, 0:ts_],
                                 AF.Copy)
                    sinc(e, 1)
        block.scalar(s_scalar)

        def s_vector(e):
            for ci, (t0, ts_) in enumerate(tch):
                p = ci % 2
                sb = ss0 + spc * ci
                RX.wait(e, ci % 2, xt_t[ci])
                e.reduce_sum(stat[0:ts_, 0:1], xt[p][0:ts_, :], axis=AXX, opt_input=False, opt_output=False)
                e.wait_ge(sem_s, sb + 1)
                e.reduce_sum(stat[0:ts_, 1:2], sqt[0:ts_, :], axis=AXX, opt_input=False, opt_output=False)
                e.drain()
                e.tensor_scalar_mul(stat[0:ts_, 2:3], stat[0:ts_, 0:1], 1.0 / H)
                e.tensor_scalar_mul(stat[0:ts_, 3:4], stat[0:ts_, 1:2], 1.0 / H)
                vwait(e)
                e.tensor_mul(stat[0:ts_, 4:5], stat[0:ts_, 2:3], stat[0:ts_, 2:3])
                vwait(e)
                e.tensor_sub(hvar[0:ts_, :], stat[0:ts_, 3:4], stat[0:ts_, 4:5])
                vinc(e, 1)
                e.wait_ge(sem_s, sb + 2)
                e.tensor_scalar(stat[0:ts_, 9:10], hvar[0:ts_, :], 1.0, 1e-5,
                                ALU.mult, ALU.add)
                e.tensor_mul(stat[0:ts_, 8:9], hrstd[0:ts_, :],
                             hrstd[0:ts_, :])
                vwait(e)
                e.tensor_mul(stat[0:ts_, 11:12], stat[0:ts_, 8:9],
                             stat[0:ts_, 9:10])
                vwait(e)
                e.tensor_scalar(stat[0:ts_, 13:14], stat[0:ts_, 11:12],
                                -0.5, 1.5, ALU.mult, ALU.add)
                vwait(e)
                e.tensor_mul(stat[0:ts_, 15:16], hrstd[0:ts_, :],
                             stat[0:ts_, 13:14])
                vwait(e)
                e.tensor_mul(stat[0:ts_, 17:18], stat[0:ts_, 2:3],
                             stat[0:ts_, 15:16])
                vwait(e)
                e.tensor_scalar_mul(stat[0:ts_, 18:19], stat[0:ts_, 17:18], -1.0)
                if ci >= 1:
                    e.wait_ge(sem_t, tt0 + 32 * ci)
                if dbg_lnt is not None and ci >= 1:
                    e.wait_ge(RLD.cs[0].h, rld0 + 16 * ci)
                vwait(e)
                e.tensor_scalar(lnt[0:ts_, :], xt[p][0:ts_, :],
                                stat[0:ts_, 15:16], stat[0:ts_, 18:19],
                                ALU.mult, ALU.add)
                if quant:
                    vwait(e)
                    e.reduce_max(stat[0:ts_, 10:11], lnt[0:ts_, :], axis=AXX,
                                 apply_absolute_value=True,
                                 opt_input=False, opt_output=False)
                    e.drain()
                    e.tensor_scalar_max(hrm[0:ts_, :], stat[0:ts_, 10:11],
                                        1e-8)
                    vinc(e, 1)
                    e.tensor_scalar_mul(stat[0:ts_, 14:15], hrm[0:ts_, :],
                                        1.0 / 127.0)
                    e.wait_ge(sem_s, sb + 3)
                    vwait(e)
                    e.tensor_mul(stat[0:ts_, 8:9], hinv[0:ts_, :],
                                 stat[0:ts_, 14:15])
                    vwait(e)
                    e.tensor_scalar(stat[0:ts_, 11:12], stat[0:ts_, 8:9], -1.0,
                                    2.0, ALU.mult, ALU.add)
                    vwait(e)
                    e.tensor_mul(stat[0:ts_, 16:17], hinv[0:ts_, :],
                                 stat[0:ts_, 11:12])
                    vwait(e)
                    e.tensor_scalar_mul(lnt[0:ts_, :], lnt[0:ts_, :],
                                        stat[0:ts_, 16:17])
                    e.tensor_scalar(lnt[0:ts_, :], lnt[0:ts_, :], MAGIC, -MAGIC,
                                    ALU.add, ALU.add)
                    e.tensor_scalar(lnt[0:ts_, :], lnt[0:ts_, :], 127.0, -127.0,
                                    ALU.min, ALU.max)
                    e.tensor_scalar_mul(lnt[0:ts_, :], lnt[0:ts_, :],
                                        stat[0:ts_, 14:15])
                    e.tensor_copy(shist[:, ci, 0:1], hvar[:, :])
                    e.tensor_copy(shist[:, ci, 1:2], hrstd[:, :])
                    e.tensor_copy(shist[:, ci, 2:3], hrm[:, :])
                    e.tensor_copy(shist[:, ci, 3:4], hinv[:, :])
                    e.tensor_copy(shist[:, ci, 4:20], stat[:, 4:20])
                    e.tensor_copy(shist[:, ci, 20:24], stat[:, 0:4])
                    vinc(e, 1)
                else:
                    e.tensor_copy(shist[:, ci, 0:1], hvar[:, :])
                    e.tensor_copy(shist[:, ci, 1:2], hrstd[:, :])
                    e.tensor_copy(shist[:, ci, 4:20], stat[:, 4:20])
                    e.tensor_copy(shist[:, ci, 20:24], stat[:, 0:4])
                    vinc(e, 1)
        block.vector(s_vector)

        if dbg_lnt is not None:
            def s_lntdump(e):
                for ci in range(len(tch)):
                    e.wait_ge(sem_v, vv0 + vpc * (ci + 1))
                    e.dma_start(out=dbg_lnt[ci], in_=lnt[:]
                                ).then_inc(RLD.cs[0].h, 16)
            block.gpsimd(s_lntdump)

        def s_tensor(e):
            for ci, (t0, ts_) in enumerate(tch):
                e.wait_ge(sem_v, vv0 + vpc * (ci + 1))
                for c in range(32):
                    gidx = 32 * ci + c
                    if gidx >= 2:
                        e.wait_ge(sem_s, evac_done(gidx - 2))
                    e.transpose(ptq[c % 2][:, 0:ts_],
                                lnt[0:ts_, c * 128:(c + 1) * 128],
                                identf[0:ts_, 0:ts_]).then_inc(sem_t, 1)
        block.tensor(s_tensor)

        if dbg_hist:
            dbg_h = nc.dram_tensor(f"dbg_hist_{tag}", [128, len(tch), 24], F32,
                                   kind="ExternalOutput").ap()
            hv = VV.n

            def s_hdump(e):
                e.wait_ge(sem_v, hv)
                e.dma_start(out=dbg_h, in_=shist[:]).then_inc(sem_st, 16)
            ST.n += 16
            block.gpsimd(s_hdump)

        barrier()
        sc.close()

    def emit_ffn(res_src, dst, tch, actT, tag):
        sc = ExitStack()
        ngrp1 = (len(tch) + 1) // 2
        grps = [tch[:ngrp1], tch[ngrp1:]]
        nps = max(len(g) for g in grps)
        wmov = [sc.enter_context(nc.sbuf_tensor(f"{tag}wm{i}", [128, 512], BF16))
                for i in range(4)]
        xres = [sc.enter_context(nc.sbuf_tensor(f"{tag}xr{i}", [128, 512], F32))
                for i in range(nps)]
        otile = [sc.enter_context(nc.sbuf_tensor(f"{tag}ot{i}", [128, 512], F32))
                 for i in range(nps)]
        psq = [sc.enter_context(nc.psum_tensor(f"{tag}ps{i}", [128, 512], F32))
               for i in range(nps)]
        glist = []
        for og in range(8):
            for g in range(2):
                glist.append((og, grps[g]))
        NG = len(glist)
        RW = Ring(0, 4)
        RXR = Ring(4, nps)
        ROT = Ring(9, nps)
        vv0, tt0 = VV.n, TT.n
        tt_at, vv_at = [], []
        tuse = []
        wm_t, xr_t, ot_t = [], [], []
        cw = [RW.cs[i].n for i in range(4)]
        cx = [RXR.cs[i].n for i in range(nps)]
        co = [ROT.cs[i].n for i in range(nps)]
        bb, cc = tt0, vv0
        for og, tl in glist:
            L = len(tl)
            tuse.append([bb + i + 1 for i in range(32)])
            wrow = []
            for i in range(32):
                cw[i % 4] += 16
                wrow.append(cw[i % 4])
            wm_t.append(wrow)
            xrow = []
            orow = []
            for k in range(L):
                cx[k] += 16
                xrow.append(cx[k])
                co[k] += 16
                orow.append(co[k])
            xr_t.append(xrow)
            ot_t.append(orow)
            bb += 32 + L
            cc += L
            tt_at.append(bb)
            vv_at.append(cc)
        TT.n, VV.n = bb, cc
        for i in range(4):
            RW.cs[i].n = cw[i]
        for k in range(nps):
            RXR.cs[k].n = cx[k]
            ROT.cs[k].n = co[k]

        def s_sync(e):
            for gi, (og, tl) in enumerate(glist):
                for i in range(32):
                    flat = gi * 32 + i
                    if flat >= 4:
                        gp, ip = divmod(flat - 4, 32)
                        e.wait_ge(sem_t, tuse[gp][ip])
                    e.dma_start(out=wmov[i % 4][:],
                                in_=wdq_bf[i * 128:(i + 1) * 128,
                                           og * 512:(og + 1) * 512]
                                ).then_inc(RW.cs[i % 4].h, 16)
                if gi >= 1:
                    e.wait_ge(sem_v, vv_at[gi - 1])
                for k, (t0, ts_) in enumerate(tl):
                    e.dma_start(out=xres[k][0:ts_, :],
                                in_=res_src[t0:t0 + ts_,
                                            og * 512:(og + 1) * 512]
                                ).then_inc(RXR.cs[k].h, 16)
        block.sync(s_sync)

        def s_tensor(e):
            for gi, (og, tl) in enumerate(glist):
                if gi >= 1:
                    e.wait_ge(sem_v, vv_at[gi - 1])
                for i in range(32):
                    e.wait_ge(RW.cs[i % 4].h, wm_t[gi][i])
                    ins = None
                    for k, (t0, ts_) in enumerate(tl):
                        ins = e.matmul(psq[k][0:ts_, :],
                                       actT[:, i, t0:t0 + ts_], wmov[i % 4][:],
                                       start=(i == 0), stop=False)
                    ins.then_inc(sem_t, 1)
                for k, (t0, ts_) in enumerate(tl):
                    e.matmul(psq[k][0:ts_, :], ones1b[0:1, 0:ts_],
                             biasrow[0:1, og * 512:(og + 1) * 512],
                             start=False, stop=True).then_inc(sem_t, 1)
        block.tensor(s_tensor)

        def s_vector(e):
            for gi, (og, tl) in enumerate(glist):
                base_t = (tt0 if gi == 0 else tt_at[gi - 1]) + 32
                for k, (t0, ts_) in enumerate(tl):
                    e.wait_ge(sem_t, base_t + k + 1)
                    e.wait_ge(RXR.cs[k].h, xr_t[gi][k])
                    if gi >= 2:
                        e.wait_ge(ROT.cs[k].h, ot_t[gi - 2][k])
                    e.scalar_tensor_tensor(otile[k][0:ts_, :], psq[k][0:ts_, :],
                                           0.5, xres[k][0:ts_, :],
                                           ALU.mult, ALU.add)
                    vinc(e, 1)
        block.vector(s_vector)

        def s_store(e):
            for gi, (og, tl) in enumerate(glist):
                base_v = vv0 if gi == 0 else vv_at[gi - 1]
                for k, (t0, ts_) in enumerate(tl):
                    e.wait_ge(sem_v, base_v + k + 1)
                    e.dma_start(out=dst[t0:t0 + ts_, og * 512:(og + 1) * 512],
                                in_=otile[k][0:ts_, :]).then_inc(ROT.cs[k].h, 16)
        block.gpsimd(s_store)

        barrier()
        sc.close()

    scA = ExitStack()
    actT = scA.enter_context(nc.sbuf_tensor("actT", [128, 32, TF], BF16))
    emit_ln_quant(x_in, TCH_F, not DBG_NOQUANT, actT, "q1", dbg_hist=debug)
    if debug:
        dbg_act1 = nc.dram_tensor("dbg_act1", [128, 32 * TF], BF16,
                                  kind="ExternalOutput").ap()

        def dbg_a1(e):
            e.dma_start(out=dbg_act1,
                        in_=actT[:].rearrange("p a b -> p (a b)")
                        ).then_inc(sem_st, 16)
        ST.n += 16
        block.gpsimd(dbg_a1)
    if stop_after == "q1":
        if debug:
            fin_st0 = ST.n

            def fin_gp0(e):
                e.wait_ge(sem_st, fin_st0)
            block.gpsimd(fin_gp0)
        blk_cm.__exit__(None, None, None)
        scA.close()
        pers.close()
        return nc
    emit_ffn(x_in, u1_scr, TCH_F, actT, "f1")
    emit_ln_quant(u1_scr, TCH_F, False, actT, "q2")

    scC = ExitStack()
    wstat = [scC.enter_context(nc.sbuf_tensor(f"pw{i}", [128, 128], BF16))
             for i in range(2)]
    xst = [scC.enter_context(nc.sbuf_tensor(f"px{i}", [128, TF], F32))
           for i in range(2)]
    xct = [scC.enter_context(nc.sbuf_tensor(f"pc{i}", [128, TLOC], F32))
           for i in range(2)]
    utf = [scC.enter_context(nc.sbuf_tensor(f"pu{i}", [128, TLOC], F32))
           for i in range(2)]
    utb = [scC.enter_context(nc.sbuf_tensor(f"pub{i}", [128, TLOC], BF16))
           for i in range(2)]
    gtile = [scC.enter_context(nc.sbuf_tensor(f"pg{i}", [128, TLOC], F32))
             for i in range(2)]
    psp = [scC.enter_context(nc.psum_tensor(f"pp{i}", [128, 512], F32))
           for i in range(6)]

    RWS = Ring(0, 2)
    RUF = Ring(2, 2)
    RUB = Ring(4, 2)
    RGT = Ring(6, 2)
    vv0, ss0, tt0 = VV.n, SS.n, TT.n
    ws_t = []
    cws = [RWS.cs[i].n for i in range(2)]
    for oc in range(128):
        row = []
        for i in range(32):
            cws[i % 2] += 16
            row.append(cws[i % 2])
        ws_t.append(row)
    for i in range(2):
        RWS.cs[i].n = cws[i]
    uf_t, ub_t, gt_t = {}, {}, {}
    cuf = [RUF.cs[i].n for i in range(2)]
    cub = [RUB.cs[i].n for i in range(2)]
    cgt = [RGT.cs[i].n for i in range(2)]
    for oc in range(128):
        p = oc % 2
        if oc < 64:
            cuf[p] += 16
            uf_t[oc] = cuf[p]
            cub[p] += 16
            ub_t[oc] = cub[p]
        else:
            cgt[p] += 16
            gt_t[oc] = cgt[p]
    for i in range(2):
        RUF.cs[i].n = cuf[i]
        RUB.cs[i].n = cub[i]
        RGT.cs[i].n = cgt[i]
    ss_at, vv_at = [], []
    c, d = ss0, vv0
    for oc in range(128):
        c += 4 if oc < 64 else 3
        d += 2 if oc < 64 else 0
        ss_at.append(c)
        vv_at.append(d)
    SS.n, VV.n = c, d
    TT.n = tt0 + 32 * 128

    def c_sync(e):
        for oc in range(128):
            for i in range(32):
                flat = 32 * oc + i
                if flat >= 2:
                    e.wait_ge(sem_t, tt0 + flat - 1)
                e.dma_start(out=wstat[i % 2][:],
                            in_=win_bf[i * 128:(i + 1) * 128,
                                       oc * 128:(oc + 1) * 128]
                            ).then_inc(RWS.cs[i % 2].h, 16)
    block.sync(c_sync)

    def c_tensor(e):
        for oc in range(128):
            pp = (oc % 2) * 3
            if oc >= 2:
                e.wait_ge(sem_s, ss_at[oc - 2] - (3 if oc - 2 < 64 else 2))
            for i in range(32):
                e.wait_ge(RWS.cs[i % 2].h, ws_t[oc][i])
                ins = None
                for k, (f0, fs) in enumerate(TFREE_F):
                    ins = e.matmul(psp[pp + k][:, 0:fs],
                                   wstat[i % 2][:],
                                   actT[:, i, f0:f0 + fs],
                                   start=(i == 0), stop=(i == 31))
                ins.then_inc(sem_t, 1)
    block.tensor(c_tensor)

    def c_scalar(e):
        for oc in range(128):
            pp = (oc % 2) * 3
            vb = vv0 if oc == 0 else vv_at[oc - 1]
            e.wait_ge(sem_t, tt0 + 32 * (oc + 1))
            if oc < 64:
                if oc >= 2:
                    e.wait_ge(sem_v, vv_at[oc - 2] - 1)
                for k, (f0, fs) in enumerate(TFREE_F):
                    ins = e.activation(xst[oc % 2][:, f0:f0 + fs],
                                       psp[pp + k][:, 0:fs], AF.Copy)
                sinc(e, 1)
                e.wait_ge(sem_v, vb + 1)
                if oc >= 2:
                    RUF.wait(e, oc % 2, uf_t[oc - 2])
                e.activation(utf[oc % 2][:], xct[oc % 2][:], AF.Silu
                             )
                sinc(e, 3)
            else:
                if oc >= 66:
                    RGT.wait(e, oc % 2, gt_t[oc - 2])
                e.activation(gtile[oc % 2][:, 0:509], psp[pp][:, 3:512], AF.Copy)
                e.activation(gtile[oc % 2][:, 509:1021], psp[pp + 1][:, 0:512],
                             AF.Copy)
                e.activation(gtile[oc % 2][:, 1021:1024], psp[pp + 2][:, 0:3],
                             AF.Copy)
                sinc(e, 3)
    block.scalar(c_scalar)

    def c_vector(e):
        for oc in range(64):
            sb = ss0 if oc == 0 else ss_at[oc - 1]
            e.wait_ge(sem_s, sb + 1)
            dg = oc
            x = xst[oc % 2]
            e.tensor_scalar_mul(x[:, 0:HALO], x[:, 0:HALO], maskc[:, 0:1])
            e.tensor_scalar(xct[oc % 2][:], x[:, 0:TLOC],
                            convwc[:, dg, 0:1], convbc[:, dg:dg + 1],
                            ALU.mult, ALU.add)
            for j in (1, 2, 3):
                ins = e.scalar_tensor_tensor(xct[oc % 2][:], x[:, j:j + TLOC],
                                             convwc[:, dg, j:j + 1],
                                             xct[oc % 2][:],
                                             ALU.mult, ALU.add)
            vinc(e, 1)
            e.wait_ge(sem_s, ss_at[oc])
            if oc >= 2:
                RUB.wait(e, oc % 2, ub_t[oc - 2])
            e.tensor_copy(utb[oc % 2][:], utf[oc % 2][:])
            vinc(e, 1)
    block.vector(c_vector)

    def c_store(e):
        for oc in range(128):
            dg = oc if oc < 64 else oc - 64
            if oc < 64:
                e.wait_ge(sem_s, ss_at[oc])
                e.dma_start(out=u_scr[dg * 128:(dg + 1) * 128, :],
                            in_=utf[oc % 2][:]).then_inc(RUF.cs[oc % 2].h, 16)
                e.wait_ge(sem_v, vv_at[oc])
                e.dma_start(out=ubf_scr[dg * 128:(dg + 1) * 128, :],
                            in_=utb[oc % 2][:]).then_inc(RUB.cs[oc % 2].h, 16)
            else:
                e.wait_ge(sem_s, ss_at[oc])
                e.dma_start(out=gate_scr[dg * 128:(dg + 1) * 128, :],
                            in_=gtile[oc % 2][:]).then_inc(RGT.cs[oc % 2].h, 16)
    block.gpsimd(c_store)

    barrier()
    scC.close()
    scA.close()

    scD = ExitStack()
    btall = scD.enter_context(nc.sbuf_tensor("btall", [128, NST, TLOC], BF16))
    ctall = scD.enter_context(nc.sbuf_tensor("ctall", [128, NST, TLOC], BF16))

    scD1 = ExitStack()
    wxt = [scD1.enter_context(nc.sbuf_tensor(f"dw{i}", [128, 288], BF16))
           for i in range(2)]
    ubt = [scD1.enter_context(nc.sbuf_tensor(f"du{i}", [128, TLOC], BF16))
           for i in range(2)]
    ssmAB = scD1.enter_context(nc.sbuf_tensor("ssmAB", [128, 2, TLOC], F32))
    ssmC = scD1.enter_context(nc.sbuf_tensor("ssmC", [32, TLOC], F32))
    ssmtok = scD1.enter_context(nc.sbuf_tensor("ssmtok", [128, 8, 288], F32))
    stat2 = scD1.enter_context(nc.sbuf_tensor("stat2", [128, 8], F32))
    sqsm = scD1.enter_context(nc.sbuf_tensor("sqsm", [128, R], F32))
    browf = scD1.enter_context(nc.sbuf_tensor("browf", [16, TLOC], BF16))
    crowf = scD1.enter_context(nc.sbuf_tensor("crowf", [16, TLOC], BF16))
    tsnT = scD1.enter_context(nc.sbuf_tensor("tsnT", [128, 2, TLOC], BF16))
    psx = [scD1.enter_context(nc.psum_tensor(f"dps{i}", [128, 512], F32))
           for i in range(6)]
    ptr2 = [scD1.enter_context(nc.psum_tensor(f"dpt{i}", [128, 128], F32))
            for i in range(2)]

    RWX = Ring(0, 2)
    RUBT = Ring(2, 2)
    tt0 = TT.n
    TT.n += 64
    wx_t, ub_t2 = [], []
    _a = [RWX.cs[i].n for i in range(2)]
    _b = [RUBT.cs[i].n for i in range(2)]
    for ic in range(64):
        _a[ic % 2] += 16
        _b[ic % 2] += 16
        wx_t.append(_a[ic % 2])
        ub_t2.append(_b[ic % 2])
    for i in range(2):
        RWX.cs[i].n = _a[i]
        RUBT.cs[i].n = _b[i]

    def d_sync(e):
        for ic in range(64):
            if ic >= 2:
                e.wait_ge(sem_t, tt0 + ic - 1)
            e.dma_start(out=wxt[ic % 2][:],
                        in_=wx_bf[ic * 128:(ic + 1) * 128, :]
                        ).then_inc(RWX.cs[ic % 2].h, 16)
            e.dma_start(out=ubt[ic % 2][:],
                        in_=ubf_scr[ic * 128:(ic + 1) * 128, :]
                        ).then_inc(RUBT.cs[ic % 2].h, 16)
    block.sync(d_sync)

    def d_tensor(e):
        for ic in range(64):
            e.wait_ge(RWX.cs[ic % 2].h, wx_t[ic])
            e.wait_ge(RUBT.cs[ic % 2].h, ub_t2[ic])
            ins = None
            for m, (m0, ms) in enumerate([(0, 128), (128, 128), (256, 32)]):
                for tf, (f0, fs) in enumerate([(0, 512), (512, 512)]):
                    ins = e.matmul(psx[m * 2 + tf][0:ms, :],
                                   wxt[ic % 2][:, m0:m0 + ms],
                                   ubt[ic % 2][:, f0:f0 + fs],
                                   start=(ic == 0), stop=(ic == 63))
            ins.then_inc(sem_t, 1)
    block.tensor(d_tensor)
    t_ssm_done = TT.n

    s_ssm_ev = SS.bump(1)

    def d_scalar1(e):
        e.wait_ge(sem_t, t_ssm_done)
        for tf, (f0, fs) in enumerate([(0, 512), (512, 512)]):
            e.activation(ssmAB[:, 0, f0:f0 + fs], psx[0 + tf][:, 0:fs], AF.Copy)
            e.activation(ssmAB[:, 1, f0:f0 + fs], psx[2 + tf][:, 0:fs], AF.Copy)
            ins = e.activation(ssmC[0:32, f0:f0 + fs], psx[4 + tf][0:32, 0:fs],
                               AF.Copy)
        sinc(e, 1)
    block.scalar(d_scalar1)

    tt_tr0 = TT.n
    TT.n += 24
    ss_tr0 = SS.n
    SS.n += 24

    def d_tensor2(e):
        e.wait_ge(sem_s, s_ssm_ev)
        k = 0
        for tt_i in range(8):
            for part in range(2):
                k += 1
                if k >= 3:
                    e.wait_ge(sem_s, ss_tr0 + k - 2)
                e.transpose(ptr2[k % 2][:, 0:128],
                            ssmAB[:, part, tt_i * 128:(tt_i + 1) * 128],
                            identf[:, :]).then_inc(sem_t, 1)
            k += 1
            if k >= 3:
                e.wait_ge(sem_s, ss_tr0 + k - 2)
            e.transpose(ptr2[k % 2][:, 0:32],
                        ssmC[0:32, tt_i * 128:(tt_i + 1) * 128],
                        identf[0:32, 0:32]).then_inc(sem_t, 1)
    block.tensor(d_tensor2)

    def d_scalar2(e):
        k = 0
        for tt_i in range(8):
            for part in range(2):
                k += 1
                e.wait_ge(sem_t, tt_tr0 + k)
                e.activation(ssmtok[:, tt_i, part * 128:(part + 1) * 128],
                             ptr2[k % 2][:, 0:128], AF.Copy)
                sinc(e, 1)
            k += 1
            e.wait_ge(sem_t, tt_tr0 + k)
            e.activation(ssmtok[:, tt_i, 256:288], ptr2[k % 2][:, 0:32],
                         AF.Copy)
            sinc(e, 1)
    block.scalar(d_scalar2)

    groups = [(0, R, 1.0 / R), (R, NST, 1.0 / NST), (R + NST, NST, 1.0 / NST)]
    vv0b, ss0b = VV.n, SS.n
    VV.n += 8 * 3 * 2
    SS.n += 8 * 3 * 2

    def d_scalar3(e):
        k = 0
        for tt_i in range(8):
            for (c0, cw, sc_) in groups:
                k += 1
                e.wait_ge(sem_s, ss_tr0 + 3 * (tt_i + 1))
                if k >= 2:
                    e.wait_ge(sem_v, vv0b + 2 * (k - 1) - 1)
                e.activation(sqsm[:, 0:cw], ssmtok[:, tt_i, c0:c0 + cw],
                             AF.Square)
                sinc(e, 1)
                e.wait_ge(sem_v, vv0b + 2 * (k - 1) + 1)
                e.activation(stat2[:, 1:2], stat2[:, 3:4], AF.Ln,
                             bias=cst[:, 1:2])
                e.activation(stat2[:, 2:3], stat2[:, 1:2], AF.Exp,
                             scale=cst[:, 3:4])
                sinc(e, 1)
    block.scalar(d_scalar3)

    def d_vector2(e):
        k = 0
        for tt_i in range(8):
            for (c0, cw, sc_) in groups:
                k += 1
                e.wait_ge(sem_s, ss0b + 2 * (k - 1) + 1)
                e.reduce_sum(stat2[:, 0:1], sqsm[:, 0:cw], axis=AXX, opt_input=False, opt_output=False)
                e.drain()
                e.tensor_scalar_mul(stat2[:, 3:4], stat2[:, 0:1], sc_)
                vinc(e, 1)
                e.wait_ge(sem_s, ss0b + 2 * k)
                vwait(e)
                e.tensor_scalar_mul(ssmtok[:, tt_i, c0:c0 + cw],
                                    ssmtok[:, tt_i, c0:c0 + cw],
                                    stat2[:, 2:3])
                vinc(e, 1)
    block.vector(d_vector2)
    vv_rms_end = VV.n

    tt0c = TT.n
    TT.n += 32
    ss0c = SS.n
    SS.n += 32

    def d_tensor3(e):
        e.wait_ge(sem_v, vv_rms_end)
        k = 0
        for tt_i in range(8):
            for cpart in range(2):
                k += 1
                if k >= 3:
                    e.wait_ge(sem_s, ss0c + k - 2)
                e.transpose(ptr2[k % 2][:, 0:128],
                            ssmtok[:, tt_i, cpart * 128:(cpart + 1) * 128],
                            identf[:, :]).then_inc(sem_t, 1)
        for tt_i in range(8):
            k += 1
            if k >= 3:
                e.wait_ge(sem_s, ss0c + k - 2)
            e.transpose(ptr2[k % 2][0:16, 0:128],
                        ssmtok[:, tt_i, R:R + NST],
                        identf[:, :]).then_inc(sem_t, 1)
        for tt_i in range(8):
            k += 1
            if k >= 3:
                e.wait_ge(sem_s, ss0c + k - 2)
            e.transpose(ptr2[k % 2][0:16, 0:128],
                        ssmtok[:, tt_i, R + NST:R + 2 * NST],
                        identf[:, :]).then_inc(sem_t, 1)
    block.tensor(d_tensor3)

    def d_scalar4(e):
        k = 0
        for tt_i in range(8):
            for cpart in range(2):
                k += 1
                e.wait_ge(sem_t, tt0c + k)
                e.activation(tsnT[:, cpart, tt_i * 128:(tt_i + 1) * 128],
                             ptr2[k % 2][:, 0:128], AF.Copy)
                sinc(e, 1)
        for tt_i in range(8):
            k += 1
            e.wait_ge(sem_t, tt0c + k)
            e.activation(browf[0:16, tt_i * 128:(tt_i + 1) * 128],
                         ptr2[k % 2][0:16, 0:128], AF.Copy)
            sinc(e, 1)
        for tt_i in range(8):
            k += 1
            e.wait_ge(sem_t, tt0c + k)
            e.activation(crowf[0:16, tt_i * 128:(tt_i + 1) * 128],
                         ptr2[k % 2][0:16, 0:128], AF.Copy)
            sinc(e, 1)
    block.scalar(d_scalar4)
    s_dsc4_end = SS.n

    st0bc = ST.n
    ST.n += 16 * 4

    def d_bcast(e):
        e.wait_ge(sem_s, s_dsc4_end)
        e.dma_start(out=brow_d, in_=browf[:]).then_inc(sem_st, 16)
        e.dma_start(out=crow_d, in_=crowf[:]).then_inc(sem_st, 16)
        e.wait_ge(sem_st, st0bc + 32)
        e.dma_start(out=btall[:],
                    in_=bass.AP(brow_d.tensor, brow_d.offset,
                                [[0, 128], [1, NST * TLOC]])
                    ).then_inc(sem_st, 16)
        e.dma_start(out=ctall[:],
                    in_=bass.AP(crow_d.tensor, crow_d.offset,
                                [[0, 128], [1, NST * TLOC]])
                    ).then_inc(sem_st, 16)
    block.gpsimd(d_bcast)

    barrier()

    wdtt = [scD1.enter_context(nc.sbuf_tensor(f"dtw{i}", [128, 128], BF16))
            for i in range(2)]
    rzt = [scD1.enter_context(nc.sbuf_tensor(f"rzt{i}", [128, 512], F32))
           for i in range(2)]
    azt = [scD1.enter_context(nc.sbuf_tensor(f"azt{i}", [128, 512], F32))
           for i in range(2)]
    ezt = [scD1.enter_context(nc.sbuf_tensor(f"ezt{i}", [128, 512], F32))
           for i in range(2)]
    lzt = [scD1.enter_context(nc.sbuf_tensor(f"lzt{i}", [128, 512], F32))
           for i in range(2)]
    dtt_o = [scD1.enter_context(nc.sbuf_tensor(f"dto{i}", [128, TLOC], F32))
             for i in range(2)]

    RWD = Ring(4, 2)
    RDO = Ring(6, 2)
    tt0, ss0, vv0 = TT.n, SS.n, VV.n
    TT.n += 2 * 64
    SS.n += 2 * 64
    VV.n += 2 * 64
    wd_t = []
    _a = [RWD.cs[i].n for i in range(2)]
    for oc in range(64):
        _a[0] += 16
        _a[1] += 16
        wd_t.append((_a[0], _a[1]))
    for i in range(2):
        RWD.cs[i].n = _a[i]
    do_t = []
    _b = [RDO.cs[i].n for i in range(2)]
    for oc in range(64):
        _b[oc % 2] += 16
        do_t.append(_b[oc % 2])
    for i in range(2):
        RDO.cs[i].n = _b[i]

    def e_sync(e):
        for oc in range(64):
            for kc in range(2):
                if oc >= 1 and kc == 0:
                    e.wait_ge(sem_t, tt0 + 2 * oc)
                e.dma_start(out=wdtt[kc][:],
                            in_=wdt_bf[kc * 128:(kc + 1) * 128,
                                       oc * 128:(oc + 1) * 128]
                            ).then_inc(RWD.cs[kc].h, 16)
    block.sync(e_sync)

    def e_tensor(e):
        for oc in range(64):
            pp = (oc % 2) * 2
            if oc >= 2:
                e.wait_ge(sem_s, ss0 + 2 * (oc - 2) + 2)
            e.wait_ge(RWD.cs[0].h, wd_t[oc][0])
            e.wait_ge(RWD.cs[1].h, wd_t[oc][1])
            for tf, (f0, fs) in enumerate([(0, 512), (512, 512)]):
                ins = None
                for kc in range(2):
                    ins = e.matmul(psx[pp + tf][:, 0:fs],
                                   wdtt[kc][:], tsnT[:, kc, f0:f0 + fs],
                                   start=(kc == 0), stop=(kc == 1))
                ins.then_inc(sem_t, 1)
    block.tensor(e_tensor)

    def e_scalar(e):
        for oc in range(64):
            pp = (oc % 2) * 2
            for tf in range(2):
                idx = 2 * oc + tf
                e.wait_ge(sem_t, tt0 + idx + 1)
                if idx >= 2:
                    e.wait_ge(sem_v, vv0 + idx - 1)
                e.activation(rzt[tf][:], psx[pp + tf][:, 0:512], AF.Relu,
                             bias=bdtc[:, oc:oc + 1])
                e.activation(azt[tf][:], psx[pp + tf][:, 0:512], AF.Abs,
                             bias=bdtc[:, oc:oc + 1])
                e.activation(ezt[tf][:], azt[tf][:], AF.Exp, scale=-1.0)
                e.activation(lzt[tf][:], ezt[tf][:], AF.Ln, bias=1.0)
                sinc(e, 1)
    block.scalar(e_scalar)

    def e_vector(e):
        for oc in range(64):
            for tf, (f0, fs) in enumerate([(0, 512), (512, 512)]):
                idx = 2 * oc + tf
                e.wait_ge(sem_s, ss0 + idx + 1)
                if oc >= 2 and tf == 0:
                    RDO.wait(e, oc % 2, do_t[oc - 2])
                e.tensor_add(dtt_o[oc % 2][:, f0:f0 + fs], rzt[tf][:],
                             lzt[tf][:])
                vinc(e, 1)
    block.vector(e_vector)

    def e_store(e):
        for oc in range(64):
            e.wait_ge(sem_v, vv0 + 2 * oc + 2)
            e.dma_start(out=dt_scr[oc * 128:(oc + 1) * 128, :],
                        in_=dtt_o[oc % 2][:]).then_inc(RDO.cs[oc % 2].h, 16)
    block.gpsimd(e_store)

    barrier()
    scD1.close()

    scD2 = ExitStack()
    dtt = [scD2.enter_context(nc.sbuf_tensor(f"sdt{i}", [128, TLOC], F32))
           for i in range(2)]
    ut = [scD2.enter_context(nc.sbuf_tensor(f"sut{i}", [128, TLOC], F32))
          for i in range(2)]
    gt_ = scD2.enter_context(nc.sbuf_tensor("sgt", [128, TLOC], F32))
    dAt = [scD2.enter_context(nc.sbuf_tensor(f"sda{i}", [128, TLOC], F32))
           for i in range(2)]
    dBu = scD2.enter_context(nc.sbuf_tensor("sdbu", [128, TLOC], F32))
    Sti = scD2.enter_context(nc.sbuf_tensor("ssti", [128, TLOC], F32))
    tmpt = scD2.enter_context(nc.sbuf_tensor("stmp", [128, TLOC], F32))
    y0t = [scD2.enter_context(nc.sbuf_tensor(f"sy0{i}", [128, TLOC], F32))
           for i in range(2)]
    Pt = [scD2.enter_context(nc.sbuf_tensor(f"sP{i}", [128, TLOC], F32))
          for i in range(2)]

    RDT = Ring(0, 2)
    RUT = Ring(2, 2)
    RPS = Ring(4, 2)
    RYS = Ring(6, 2)
    vv0, ss0 = VV.n, SS.n
    SS.n += 16 * 64
    VV.n += 18 * 64
    dt_t, ut_t, ps_t, ys_t = [], [], [], []
    _a = [RDT.cs[i].n for i in range(2)]
    _b = [RUT.cs[i].n for i in range(2)]
    _c = [RPS.cs[i].n for i in range(2)]
    _d = [RYS.cs[i].n for i in range(2)]
    for dg in range(64):
        p = dg % 2
        _a[p] += 16; dt_t.append(_a[p])
        _b[p] += 16; ut_t.append(_b[p])
        _c[p] += 16; ps_t.append(_c[p])
        _d[p] += 16; ys_t.append(_d[p])
    for i in range(2):
        RDT.cs[i].n = _a[i]; RUT.cs[i].n = _b[i]
        RPS.cs[i].n = _c[i]; RYS.cs[i].n = _d[i]

    def f_sync(e):
        for dg in range(64):
            if dg >= 2:
                e.wait_ge(sem_v, vv0 + 18 * (dg - 1))
            e.dma_start(out=dtt[dg % 2][:],
                        in_=dt_scr[dg * 128:(dg + 1) * 128, :]
                        ).then_inc(RDT.cs[dg % 2].h, 16)
            e.dma_start(out=ut[dg % 2][:],
                        in_=u_scr[dg * 128:(dg + 1) * 128, :]
                        ).then_inc(RUT.cs[dg % 2].h, 16)
    block.sync(f_sync)

    def f_scalar(e):
        for dg in range(64):
            e.wait_ge(RDT.cs[dg % 2].h, dt_t[dg])
            for n in range(NST):
                idx = dg * NST + n
                if idx >= 2:
                    dgp, np_ = divmod(idx - 2, NST)
                    e.wait_ge(sem_v, vv0 + 18 * dgp + 1 + np_ + 1)
                e.activation(dAt[n % 2][:], dtt[dg % 2][:], AF.Exp,
                             scale=acol[:, idx:idx + 1])
                sinc(e, 1)
    block.scalar(f_scalar)

    def f_vector(e):
        for dg in range(64):
            e.wait_ge(RDT.cs[dg % 2].h, dt_t[dg])
            e.wait_ge(RUT.cs[dg % 2].h, ut_t[dg])
            if dg >= 2:
                RPS.wait(e, dg % 2, ps_t[dg - 2])
                RYS.wait(e, dg % 2, ys_t[dg - 2])
            e.tensor_tensor_scan(Pt[dg % 2][:], onest[:], dtt[dg % 2][:], 0.0,
                                 ALU.mult, ALU.add)
            e.tensor_mul(gt_[:], dtt[dg % 2][:], ut[dg % 2][:])
            e.tensor_copy(ptotc[:, dg:dg + 1], Pt[dg % 2][:, TLOC - 1:TLOC]
                          )
            vinc(e, 1)
            for n in range(NST):
                e.wait_ge(sem_s, ss0 + 16 * dg + n + 1)
                e.tensor_mul(dBu[:], gt_[:], btall[:, n, :])
                e.tensor_tensor_scan(Sti[:], dAt[n % 2][:], dBu[:], 0.0,
                                     ALU.mult, ALU.add)
                vinc(e, 1)
                e.tensor_mul(tmpt[:], Sti[:], ctall[:, n, :])
                e.tensor_copy(zall[:, dg * NST + n:dg * NST + n + 1],
                              Sti[:, TLOC - 1:TLOC])
                if n == 0:
                    ins = e.tensor_copy(y0t[dg % 2][:], tmpt[:])
                else:
                    ins = e.tensor_add(y0t[dg % 2][:], y0t[dg % 2][:], tmpt[:])
            vinc(e, 1)
    block.vector(f_vector)

    def f_store(e):
        for dg in range(64):
            e.wait_ge(sem_v, vv0 + 18 * (dg + 1))
            e.dma_start(out=p_scr[dg * 128:(dg + 1) * 128, :],
                        in_=Pt[dg % 2][:]).then_inc(RPS.cs[dg % 2].h, 16)
            e.dma_start(out=y0_scr[dg * 128:(dg + 1) * 128, :],
                        in_=y0t[dg % 2][:]).then_inc(RYS.cs[dg % 2].h, 16)
    block.gpsimd(f_store)

    vv_f_end = VV.n
    v_dtot = VV.bump(1)
    s_dtot = SS.bump(1)

    def f2_vector(e):
        e.wait_ge(sem_v, vv_f_end)
        prep3 = dtotT[:].rearrange("p (g n) -> p g n", n=NST)
        for n in range(NST):
            e.tensor_copy(prep3[:, :, n], ptotc[:, :])
        vwait(e)
        e.tensor_mul(dtotT[:], dtotT[:], acol[:])
        vinc(e, 1)
    block.vector(f2_vector)

    def f2_scalar(e):
        e.wait_ge(sem_v, v_dtot)
        e.activation(dtotT[:], dtotT[:], AF.Exp)
        sinc(e, 1)
    block.scalar(f2_scalar)

    barrier()

    zg = scD2.enter_context(nc.sbuf_tensor("zg", [128, TLOC], F32))
    dgt = scD2.enter_context(nc.sbuf_tensor("dgt", [128, TLOC], F32))
    spre = scD2.enter_context(nc.sbuf_tensor("spre", [128, TLOC], F32))
    RZG = Ring(8, 1)
    RDG = Ring(9, 1)
    st0, vv0 = ST.n, VV.n
    ST.n += 32
    cc1 = CC.bump(1)
    VV.n += NCORES
    zg_t = [RZG.cs[0].n + 16 * (j + 1) for j in range(NCORES)]
    dg_t = [RDG.cs[0].n + 16 * (j + 1) for j in range(NCORES)]
    RZG.cs[0].n += 16 * NCORES
    RDG.cs[0].n += 16 * NCORES

    def g_gpsimd(e):
        e.dma_start(out=ag_in_t.ap()[0], in_=zall[:]).then_inc(sem_st, 16)
        e.dma_start(out=ag_in_t.ap()[1], in_=dtotT[:]).then_inc(sem_st, 16)
        e.wait_ge(sem_st, st0 + 32)
        e.collective_compute(
            "AllGather", ALU.bypass,
            replica_groups=[list(range(NCORES))],
            ins=[ag_in_t.ap()],
            outs=[ag_out_t.ap()],
        ).then_inc(sem_cc, 1)
    block.gpsimd(g_gpsimd)

    def g_sync(e):
        e.wait_ge(sem_cc, cc1)
        for j in range(NCORES):
            if j >= 1:
                e.wait_ge(sem_v, vv0 + j)
            e.dma_start(out=zg[:], in_=ag_out_t.ap()[j, 0]
                        ).then_inc(RZG.cs[0].h, 16)
            e.dma_start(out=dgt[:], in_=ag_out_t.ap()[j, 1]
                        ).then_inc(RDG.cs[0].h, 16)
    block.sync(g_sync)

    def g_vector(e):
        e.memset(spre[:], 0.0)
        e.memset(sinT[:], 0.0)
        for j in range(NCORES):
            e.wait_ge(RZG.cs[0].h, zg_t[j])
            e.wait_ge(RDG.cs[0].h, dg_t[j])
            e.tensor_scalar_mul(tmpt[:], spre[:], oh[:, j:j + 1])
            e.tensor_add(sinT[:], sinT[:], tmpt[:])
            e.tensor_mul(spre[:], spre[:], dgt[:])
            e.tensor_add(spre[:], spre[:], zg[:])
            vinc(e, 1)
    block.vector(g_vector)

    barrier()

    Et = [scD2.enter_context(nc.sbuf_tensor(f"hE{i}", [128, TLOC], F32))
          for i in range(2)]
    ECt = scD2.enter_context(nc.sbuf_tensor("hEC", [128, TLOC], F32))
    gat = [scD2.enter_context(nc.sbuf_tensor(f"hg{i}", [128, TLOC], F32))
           for i in range(2)]
    sgt2 = scD2.enter_context(nc.sbuf_tensor("hsg", [128, TLOC], F32))
    yft = [scD2.enter_context(nc.sbuf_tensor(f"hyf{i}", [128, TLOC], BF16))
           for i in range(2)]

    RHY = Ring(0, 2)
    RHP = Ring(2, 2)
    RHU = Ring(4, 2)
    RHG = Ring(6, 2)
    RYF = Ring(8, 2)
    vv0, ss0 = VV.n, SS.n
    SS.n += 17 * 64
    VV.n += 18 * 64
    hy_t, hp_t, hu_t, hg_t, yf_t = [], [], [], [], []
    _r = [[rr.cs[i].n for i in range(2)] for rr in (RHY, RHP, RHU, RHG, RYF)]
    for dg in range(64):
        p = dg % 2
        for ri, tl in enumerate((hy_t, hp_t, hu_t, hg_t, yf_t)):
            _r[ri][p] += 16
            tl.append(_r[ri][p])
    for ri, rr in enumerate((RHY, RHP, RHU, RHG, RYF)):
        for i in range(2):
            rr.cs[i].n = _r[ri][i]

    def h_sync(e):
        for dg in range(64):
            if dg >= 2:
                e.wait_ge(sem_v, vv0 + 18 * (dg - 1))
            e.dma_start(out=y0t[dg % 2][:],
                        in_=y0_scr[dg * 128:(dg + 1) * 128, :]
                        ).then_inc(RHY.cs[dg % 2].h, 16)
            e.dma_start(out=Pt[dg % 2][:],
                        in_=p_scr[dg * 128:(dg + 1) * 128, :]
                        ).then_inc(RHP.cs[dg % 2].h, 16)
            e.dma_start(out=ut[dg % 2][:],
                        in_=u_scr[dg * 128:(dg + 1) * 128, :]
                        ).then_inc(RHU.cs[dg % 2].h, 16)
            e.dma_start(out=gat[dg % 2][:],
                        in_=gate_scr[dg * 128:(dg + 1) * 128, :]
                        ).then_inc(RHG.cs[dg % 2].h, 16)
    block.sync(h_sync)

    def h_scalar(e):
        for dg in range(64):
            e.wait_ge(RHP.cs[dg % 2].h, hp_t[dg])
            e.wait_ge(RHG.cs[dg % 2].h, hg_t[dg])
            for n in range(NST):
                idx = dg * NST + n
                if idx >= 2:
                    dgp, np_ = divmod(idx - 2, NST)
                    e.wait_ge(sem_v, vv0 + 18 * dgp + np_ + 1)
                e.activation(Et[n % 2][:], Pt[dg % 2][:], AF.Exp,
                             scale=acol[:, idx:idx + 1])
                sinc(e, 1)
            if dg >= 1:
                e.wait_ge(sem_v, vv0 + 18 * dg)
            e.activation(sgt2[:], gat[dg % 2][:], AF.Silu)
            sinc(e, 1)
    block.scalar(h_scalar)

    def h_vector(e):
        for dg in range(64):
            e.wait_ge(RHY.cs[dg % 2].h, hy_t[dg])
            e.wait_ge(RHU.cs[dg % 2].h, hu_t[dg])
            if dg >= 2:
                RYF.wait(e, dg % 2, yf_t[dg - 2])
            for n in range(NST):
                e.wait_ge(sem_s, ss0 + 17 * dg + n + 1)
                col = dg * NST + n
                e.tensor_mul(ECt[:], Et[n % 2][:], ctall[:, n, :])
                e.scalar_tensor_tensor(y0t[dg % 2][:], ECt[:],
                                       sinT[:, col:col + 1], y0t[dg % 2][:],
                                       ALU.mult, ALU.add)
                vinc(e, 1)
            e.scalar_tensor_tensor(y0t[dg % 2][:], ut[dg % 2][:],
                                   dcolv[:, dg:dg + 1], y0t[dg % 2][:],
                                   ALU.mult, ALU.add)
            vinc(e, 1)
            e.wait_ge(sem_s, ss0 + 17 * dg + 17)
            e.tensor_mul(yft[dg % 2][:], y0t[dg % 2][:], sgt2[:]
                         )
            vinc(e, 1)
    block.vector(h_vector)

    def h_store(e):
        for dg in range(64):
            e.wait_ge(sem_v, vv0 + 18 * (dg + 1))
            e.dma_start(out=yf_scr[dg * 128:(dg + 1) * 128, :],
                        in_=yft[dg % 2][:]).then_inc(RYF.cs[dg % 2].h, 16)
    block.gpsimd(h_store)

    barrier()
    scD2.close()
    scD.close()

    scI = ExitStack()
    ystat = [scI.enter_context(nc.sbuf_tensor(f"iy{i}", [128, TLOC], BF16))
             for i in range(2)]
    wmov2 = [scI.enter_context(nc.sbuf_tensor(f"iw{i}", [128, 512], BF16))
             for i in range(2)]
    u1res = [scI.enter_context(nc.sbuf_tensor(f"iu{i}", [128, 512], F32))
             for i in range(8)]
    u2t = [scI.enter_context(nc.sbuf_tensor(f"io{i}", [128, 512], F32))
           for i in range(8)]
    pso = [scI.enter_context(nc.psum_tensor(f"ip{i}", [128, 512], F32))
           for i in range(8)]

    RWO = Ring(0, 2)
    RYT = Ring(2, 2)
    RU1 = Ring(4, 8)
    vv0, tt0, st0 = VV.n, TT.n, ST.n
    TT.n += 8 * 64
    VV.n += 8 * 8
    ST.n += 8 * 16 * 8
    wo_t, yt_t = [], []
    _a = [RWO.cs[i].n for i in range(2)]
    _b = [RYT.cs[i].n for i in range(2)]
    for og in range(8):
        ra, rb = [], []
        for ic in range(64):
            _a[ic % 2] += 16
            _b[ic % 2] += 16
            ra.append(_a[ic % 2])
            rb.append(_b[ic % 2])
        wo_t.append(ra)
        yt_t.append(rb)
    for i in range(2):
        RWO.cs[i].n = _a[i]
        RYT.cs[i].n = _b[i]
    u1_t = []
    _c = [RU1.cs[i].n for i in range(8)]
    for og in range(8):
        row = []
        for t in range(8):
            _c[t] += 16
            row.append(_c[t])
        u1_t.append(row)
    for i in range(8):
        RU1.cs[i].n = _c[i]

    def i_sync(e):
        for og in range(8):
            for ic in range(64):
                flat = 64 * og + ic
                if flat >= 2:
                    e.wait_ge(sem_t, tt0 + flat - 1)
                e.dma_start(out=wmov2[ic % 2][:],
                            in_=wout_bf[ic * 128:(ic + 1) * 128,
                                        og * 512:(og + 1) * 512]
                            ).then_inc(RWO.cs[ic % 2].h, 16)
                e.dma_start(out=ystat[ic % 2][:],
                            in_=yf_scr[ic * 128:(ic + 1) * 128, :]
                            ).then_inc(RYT.cs[ic % 2].h, 16)
            if og >= 1:
                e.wait_ge(sem_v, vv0 + 8 * og)
            for t in range(8):
                e.dma_start(out=u1res[t][:],
                            in_=u1_scr[HALO + t * 128:HALO + (t + 1) * 128,
                                       og * 512:(og + 1) * 512]
                            ).then_inc(RU1.cs[t].h, 16)
    block.sync(i_sync)

    def i_tensor(e):
        for og in range(8):
            if og >= 1:
                e.wait_ge(sem_v, vv0 + 8 * og)
            for ic in range(64):
                e.wait_ge(RWO.cs[ic % 2].h, wo_t[og][ic])
                e.wait_ge(RYT.cs[ic % 2].h, yt_t[og][ic])
                ins = None
                for t in range(8):
                    ins = e.matmul(pso[t][:, :],
                                   ystat[ic % 2][:, t * 128:(t + 1) * 128],
                                   wmov2[ic % 2][:],
                                   start=(ic == 0), stop=(ic == 63))
                ins.then_inc(sem_t, 1)
    block.tensor(i_tensor)

    def i_vector(e):
        for og in range(8):
            for t in range(8):
                e.wait_ge(sem_t, tt0 + 64 * (og + 1))
                e.wait_ge(RU1.cs[t].h, u1_t[og][t])
                if og >= 1:
                    e.wait_ge(sem_st, st0 + 16 * 8 * og)
                e.tensor_add(u2t[t][:], pso[t][:, :], u1res[t][:]
                             )
                vinc(e, 1)
    block.vector(i_vector)

    def i_store(e):
        for og in range(8):
            for t in range(8):
                e.wait_ge(sem_v, vv0 + 8 * og + t + 1)
                e.dma_start(out=u2_scr[t * 128:(t + 1) * 128,
                                       og * 512:(og + 1) * 512],
                            in_=u2t[t][:]).then_inc(sem_st, 16)
    block.gpsimd(i_store)

    barrier()
    scI.close()

    scJ = ExitStack()
    actT2 = scJ.enter_context(nc.sbuf_tensor("actT2", [128, 32, TLOC], BF16))
    emit_ln_quant(u2_scr, TCH_R, True, actT2, "q3")
    emit_ffn(u2_scr, out_e, TCH_R, actT2, "f2")
    scJ.close()

    if debug:
        dbgs = [("dbg_u1", u1_scr, [TF, H], F32),
                ("dbg_u", u_scr, [DIN, TLOC], F32),
                ("dbg_gate", gate_scr, [DIN, TLOC], F32),
                ("dbg_dt", dt_scr, [DIN, TLOC], F32),
                ("dbg_p", p_scr, [DIN, TLOC], F32),
                ("dbg_y0", y0_scr, [DIN, TLOC], F32),
                ("dbg_yf", yf_scr, [DIN, TLOC], BF16),
                ("dbg_u2", u2_scr, [TLOC, H], F32),
                ("dbg_wdq", wdq_bf, [H, H], BF16),
                ("dbg_win", win_bf, [H, 2 * DIN], BF16)]
        dbg_aps = [(nc.dram_tensor(nm, sh, dt_, kind="ExternalOutput").ap(), srcd)
                   for nm, srcd, sh, dt_ in dbgs]

        def dbg_gp(e):
            for dst, srcd in dbg_aps:
                e.dma_start(out=dst, in_=srcd).then_inc(sem_st, 16)
        ST.n += 16 * len(dbg_aps)
        block.gpsimd(dbg_gp)

    fin_st = ST.n

    def fin_gpsimd(e):
        e.wait_ge(sem_st, fin_st)
    block.gpsimd(fin_gpsimd)

    blk_cm.__exit__(None, None, None)
    pers.close()
    return nc


_NC_CACHE = {}


def kernel(L0_attn_hidden_states, L0_attn_W_in, L0_attn_conv_w, L0_attn_conv_b,
           L0_attn_W_x, L0_attn_W_dt, L0_attn_b_dt, L0_attn_A_log, L0_attn_D,
           L0_attn_W_out, L0_ffn_W_int, L0_ffn_scale_i32, L0_ffn_bias):
    x = np.ascontiguousarray(np.asarray(L0_attn_hidden_states)[0])
    w_in_t = np.ascontiguousarray(np.asarray(L0_attn_W_in).T)
    w_out_t = np.ascontiguousarray(np.asarray(L0_attn_W_out).T)
    w_int_t = np.ascontiguousarray(np.asarray(L0_ffn_W_int).T)
    scale_rep = np.ascontiguousarray(
        np.repeat(np.asarray(L0_ffn_scale_i32).T, 64, axis=0))
    w_x_t = np.ascontiguousarray(np.asarray(L0_attn_W_x).T)
    w_dt_t = np.ascontiguousarray(np.asarray(L0_attn_W_dt).T)
    identf = np.eye(128, dtype=np.float32)

    if "nc" not in _NC_CACHE:
        _NC_CACHE["nc"] = build_nc()
    nc = _NC_CACHE["nc"]

    shared = {
        "w_in_t": w_in_t, "w_out_t": w_out_t, "w_int_t": w_int_t,
        "scale_rep": scale_rep, "w_x_t": w_x_t, "w_dt_t": w_dt_t,
        "convw": np.ascontiguousarray(
            np.asarray(L0_attn_conv_w).reshape(64, 128, 4).transpose(1, 0, 2)),
        "convb": np.ascontiguousarray(
            np.asarray(L0_attn_conv_b).reshape(64, 128).T),
        "bdt": np.ascontiguousarray(np.asarray(L0_attn_b_dt).reshape(64, 128).T),
        "alog": np.ascontiguousarray(
            np.asarray(L0_attn_A_log).reshape(64, 128, NST).transpose(1, 0, 2)
            .reshape(128, 64 * NST)),
        "dvec": np.ascontiguousarray(np.asarray(L0_attn_D).reshape(64, 128).T),
        "fbias": np.ascontiguousarray(np.asarray(L0_ffn_bias).reshape(1, H)),
        "identf": identf,
    }
    in_maps = []
    for c in range(NCORES):
        t0 = c * TLOC
        if c == 0:
            xs = np.concatenate([np.zeros((HALO, H), np.float32), x[0:TLOC]], 0)
        else:
            xs = x[t0 - HALO:t0 + TLOC]
        m = dict(shared)
        m["x"] = np.ascontiguousarray(xs)
        m["mask"] = np.full((128, 1), 0.0 if c == 0 else 1.0, np.float32)
        onehot = np.zeros((128, NCORES), np.float32)
        onehot[:, c] = 1.0
        m["onehot"] = onehot
        in_maps.append(m)

    res = run_bass_kernel_spmd(nc, in_maps, core_ids=list(range(NCORES)))
    _NC_CACHE["last_results"] = res.results
    out = np.concatenate([r["out"] for r in res.results], axis=0)
    return out.reshape(1, 8192, H)


# version 02
# speedup vs baseline: 1.1849x; 1.1849x over previous
import numpy as np
import ml_dtypes
from contextlib import ExitStack

import concourse.bass as bass
import concourse.mybir as mybir
from concourse.bass_utils import run_bass_kernel_spmd

F32 = mybir.dt.float32
BF16 = mybir.dt.bfloat16
I32 = mybir.dt.int32
AF = mybir.ActivationFunctionType
ALU = mybir.AluOpType
AXX = mybir.AxisListType.X

NCORES = 8
H = 4096
DIN = 8192
NST = 16
R = 256
TLOC = 1024
HALO = 3
TF = TLOC + HALO
MAGIC = float(np.float32(1.5 * 2 ** 23))
LN127 = float(np.log(127.0))
SC24 = float(2.0 ** -24)

TCH_F = [(i * 128, 128) for i in range(8)] + [(1024, 3)]
DBG_NOQUANT = False
TCH_R = [(i * 128, 128) for i in range(8)]
TFREE_F = [(0, 512), (512, 512), (1024, 3)]


class Ctr:
    def __init__(self, h):
        self.h = h
        self.n = 0

    def bump(self, k=1):
        self.n += k
        return self.n


def build_nc(debug=False, stop_after=None):
    nc = bass.Bass("TRN2", target_bir_lowering=False)

    x_in = nc.dram_tensor("x", [TF, H], F32, kind="ExternalInput").ap()
    out_e = nc.dram_tensor("out", [TLOC, H], F32, kind="ExternalOutput").ap()
    w_in_t = nc.dram_tensor("w_in_t", [H, 2 * DIN], F32, kind="ExternalInput").ap()
    w_out_t = nc.dram_tensor("w_out_t", [DIN, H], F32, kind="ExternalInput").ap()
    w_int_t = nc.dram_tensor("w_int_t", [H, H], I32, kind="ExternalInput").ap()
    scale_rep = nc.dram_tensor("scale_rep", [H, H], I32, kind="ExternalInput").ap()
    w_x_t = nc.dram_tensor("w_x_t", [DIN, 288], F32, kind="ExternalInput").ap()
    w_dt_t = nc.dram_tensor("w_dt_t", [R, DIN], F32, kind="ExternalInput").ap()
    convw_in = nc.dram_tensor("convw", [128, 64, 4], F32, kind="ExternalInput").ap()
    convb_in = nc.dram_tensor("convb", [128, 64], F32, kind="ExternalInput").ap()
    bdt_in = nc.dram_tensor("bdt", [128, 64], F32, kind="ExternalInput").ap()
    alog_in = nc.dram_tensor("alog", [128, 64 * NST], F32, kind="ExternalInput").ap()
    dvec_in = nc.dram_tensor("dvec", [128, 64], F32, kind="ExternalInput").ap()
    fbias_in = nc.dram_tensor("fbias", [1, H], F32, kind="ExternalInput").ap()
    mask_in = nc.dram_tensor("mask", [128, 1], F32, kind="ExternalInput").ap()
    oh_in = nc.dram_tensor("onehot", [128, NCORES], F32, kind="ExternalInput").ap()
    identf_in = nc.dram_tensor("identf", [128, 128], F32, kind="ExternalInput").ap()

    wdq_bf = nc.dram_tensor("wdq_bf", [8, 128, 32, 512], BF16).ap()
    win_bf = nc.dram_tensor("win_bf", [128, 128, 32, 128], BF16).ap()
    wout_bf = nc.dram_tensor("wout_bf", [8, 128, 64, 512], BF16).ap()
    wx_bf = nc.dram_tensor("wx_bf", [DIN, 288], BF16).ap()
    wdt_bf = nc.dram_tensor("wdt_bf", [R, DIN], BF16).ap()
    u_scr = nc.dram_tensor("u_scr", [DIN, TLOC], F32).ap()
    ubf_scr = nc.dram_tensor("ubf_scr", [DIN, TLOC], BF16).ap()
    gate_scr = nc.dram_tensor("gate_scr", [DIN, TLOC], F32).ap()
    dt_scr = nc.dram_tensor("dt_scr", [DIN, TLOC], F32).ap()
    p_scr = nc.dram_tensor("p_scr", [DIN, TLOC], F32).ap()
    y0_scr = nc.dram_tensor("y0_scr", [DIN, TLOC], F32).ap()
    yf_scr = nc.dram_tensor("yf_scr", [DIN, TLOC], BF16).ap()
    u1_scr = nc.dram_tensor("u1_scr", [TF, H], F32).ap()
    u2_scr = nc.dram_tensor("u2_scr", [TLOC, H], F32).ap()
    brow_d = nc.dram_tensor("brow_d", [NST, TLOC], BF16).ap()
    crow_d = nc.dram_tensor("crow_d", [NST, TLOC], BF16).ap()
    ag_in_t = nc.dram_tensor("ag_in", [2, 128, 1024], F32)
    ag_out_t = nc.dram_tensor("ag_out", [NCORES, 2, 128, 1024], F32)

    pers = ExitStack()

    def sbp(name, shape, dtype):
        return pers.enter_context(nc.sbuf_tensor(name, shape, dtype))

    identf = sbp("sb_identf", [128, 128], F32)
    maskc = sbp("maskc", [128, 1], F32)
    oh = sbp("oh", [128, NCORES], F32)
    acol = sbp("acol", [128, 64 * NST], F32)
    dcolv = sbp("dcolv", [128, 64], F32)
    convwc = sbp("convwc", [128, 64, 4], F32)
    convbc = sbp("convbc", [128, 64], F32)
    bdtc = sbp("bdtc", [128, 64], F32)
    ptotc = sbp("ptotc", [128, 64], F32)
    zall = sbp("zall", [128, 64 * NST], F32)
    dtotT = sbp("dtotT", [128, 64 * NST], F32)
    sinT = sbp("sinT", [128, 64 * NST], F32)
    biasrow = sbp("biasrow", [1, H], BF16)
    fbrow = sbp("fbrow", [1, H], F32)
    ones1b = sbp("ones1b", [1, 128], BF16)
    ones1f = sbp("ones1f", [1, 128], F32)
    onest = sbp("onest", [128, TLOC], F32)
    cst = sbp("cst", [128, 5], F32)
    vdm = sbp("vdm", [1, 2], F32)
    sdm = sbp("sdm", [1, 2], F32)

    blk_cm = nc.Block()
    block = blk_cm.__enter__()

    sem_ld = pers.enter_context(nc.semaphore("sm_ld"))
    sem_st = pers.enter_context(nc.semaphore("sm_st"))
    sem_t = pers.enter_context(nc.semaphore("sm_t"))
    sem_v = pers.enter_context(nc.semaphore("sm_v"))
    sem_s = pers.enter_context(nc.semaphore("sm_s"))
    sem_cc = pers.enter_context(nc.semaphore("sm_cc"))
    sem_bar = pers.enter_context(nc.semaphore("sm_bar"))
    sem_vb = pers.enter_context(nc.semaphore("sm_vb"))
    ring_sems = [pers.enter_context(nc.semaphore(f"sm_r{i}")) for i in range(20)]
    LD, ST, TT, VV, SS, CC, BAR = (Ctr(sem_ld), Ctr(sem_st), Ctr(sem_t),
                                   Ctr(sem_v), Ctr(sem_s), Ctr(sem_cc),
                                   Ctr(sem_bar))
    VB = Ctr(sem_vb)
    RC = [Ctr(s) for s in ring_sems]

    class Ring:

        def __init__(self, base, n):
            self.cs = RC[base:base + n]

        def load(self, e, slot, out, in_):
            e.dma_start(out=out, in_=in_).then_inc(self.cs[slot].h, 16)
            self.cs[slot].bump(16)
            return self.cs[slot].n

        def wait(self, e, slot, val=None):
            e.wait_ge(self.cs[slot].h, self.cs[slot].n if val is None else val)

    def vwait(e):
        e.tensor_copy(vdm[0:1, 0:1], vdm[0:1, 1:2]).then_inc(sem_vb, 1)
        e.wait_ge(sem_vb, VB.bump(1))

    def vinc(e, k=1):
        e.tensor_copy(vdm[0:1, 0:1], vdm[0:1, 1:2]).then_inc(sem_v, k)

    def sinc(e, k=1):
        e.activation(sdm[0:1, 0:1], sdm[0:1, 1:2], AF.Copy).then_inc(sem_s, k)

    def barrier():
        tgt = BAR.n + 5
        BAR.n = tgt
        snap = (LD.n, ST.n, TT.n, VV.n, SS.n)
        for name in ("sync", "vector", "scalar", "tensor", "gpsimd"):
            def f(e, tgt=tgt, name=name, snap=snap,
                  rsnap=[c.n for c in RC]):
                if name == "gpsimd":
                    for sem, v in zip((sem_ld, sem_st, sem_t, sem_v, sem_s), snap):
                        e.wait_ge(sem, v)
                    for c, v in zip(RC, rsnap):
                        if v:
                            e.wait_ge(c.h, v)
                e.sem_inc(sem_bar, 1)
                e.wait_ge(sem_bar, tgt)
            getattr(block, name)(f)

    p0 = ExitStack()
    fin = [p0.enter_context(nc.sbuf_tensor(f"fin{i}", [128, 8192], F32))
           for i in range(2)]
    bout = [p0.enter_context(nc.sbuf_tensor(f"bout{i}", [128, 8192], BF16))
            for i in range(2)]
    wint = [p0.enter_context(nc.sbuf_tensor(f"wint{i}", [128, 2048], I32))
            for i in range(2)]
    wsc = [p0.enter_context(nc.sbuf_tensor(f"wsc{i}", [128, 2048], I32))
           for i in range(2)]
    wfc = [p0.enter_context(nc.sbuf_tensor(f"wfc{i}", [128, 2048], F32))
           for i in range(2)]
    sfc = [p0.enter_context(nc.sbuf_tensor(f"sfc{i}", [128, 2048], F32))
           for i in range(2)]

    jobs = []
    for r0 in range(0, H, 128):
        ic = r0 // 128
        jobs.append(("f", w_in_t[r0:r0 + 128, 0:8192],
                     win_bf[0:64, :, ic, :].transpose([1, 0, 2]), 8192))
        jobs.append(("f", w_in_t[r0:r0 + 128, 8192:16384],
                     win_bf[64:128, :, ic, :].transpose([1, 0, 2]), 8192))
    for r0 in range(0, DIN, 128):
        ic = r0 // 128
        jobs.append(("f", w_out_t[r0:r0 + 128, :],
                     wout_bf[0:8, :, ic, :].transpose([1, 0, 2]), 4096))
    for r0 in range(0, DIN, 128):
        jobs.append(("f", w_x_t[r0:r0 + 128, :], wx_bf[r0:r0 + 128, :], 288))
    for r0 in range(0, R, 128):
        jobs.append(("f", w_dt_t[r0:r0 + 128, :], wdt_bf[r0:r0 + 128, :], DIN))
    for r0 in range(0, H, 128):
        for c0 in range(0, H, 2048):
            jobs.append(("q", r0, c0, 2048))
    NJ = len(jobs)

    NSMALL = 9
    ld0, vv0, ss0 = LD.n, VV.n, SS.n
    RF = Ring(0, 2)
    RQ = Ring(2, 2)
    RBS = Ring(4, 2)
    rf_t, rq_t, rbs_t = {}, {}, {}
    cf = [RF.cs[i].n for i in range(2)]
    cq = [RQ.cs[i].n for i in range(2)]
    cbs = [RBS.cs[i].n for i in range(2)]
    for j, jb in enumerate(jobs):
        p = j % 2
        if jb[0] == "f":
            cf[p] += 16
            rf_t[j] = cf[p]
        else:
            cq[p] += 32
            rq_t[j] = cq[p]
        cbs[p] += 16
        rbs_t[j] = cbs[p]
    ss_at = []
    a = ss0 + 1
    for jb in jobs:
        if jb[0] == "q":
            a += 2
        ss_at.append(a)
    SS.n = a
    vdone = [vv0 + 1 + j + 1 for j in range(NJ)]
    VV.n = vv0 + 1 + NJ + 1
    LD.n = ld0 + 16 * NSMALL

    def p0_sync(e):
        e.dma_start(out=identf[:], in_=identf_in).then_inc(sem_ld, 16)
        e.dma_start(out=maskc[:], in_=mask_in).then_inc(sem_ld, 16)
        e.dma_start(out=oh[:], in_=oh_in).then_inc(sem_ld, 16)
        e.dma_start(out=acol[:], in_=alog_in).then_inc(sem_ld, 16)
        e.dma_start(out=dcolv[:], in_=dvec_in).then_inc(sem_ld, 16)
        e.dma_start(out=convwc[:], in_=convw_in).then_inc(sem_ld, 16)
        e.dma_start(out=convbc[:], in_=convb_in).then_inc(sem_ld, 16)
        e.dma_start(out=bdtc[:], in_=bdt_in).then_inc(sem_ld, 16)
        e.dma_start(out=fbrow[:], in_=fbias_in).then_inc(sem_ld, 16)
        for j, jb in enumerate(jobs):
            p = j % 2
            if j >= 2:
                e.wait_ge(sem_v, vdone[j - 2])
            if jb[0] == "f":
                RF.load(e, p, fin[p][:, :jb[3]], jb[1])
            else:
                r0, c0 = jb[1], jb[2]
                RQ.load(e, p, wint[p][:], w_int_t[r0:r0 + 128, c0:c0 + 2048])
                RQ.load(e, p, wsc[p][:], scale_rep[r0:r0 + 128, c0:c0 + 2048])
    block.sync(p0_sync)
    base_small = ld0 + 16 * NSMALL

    def p0_scalar(e):
        e.wait_ge(sem_ld, base_small)
        e.activation(acol[:], acol[:], AF.Exp)
        sinc(e, 1)
        for j, jb in enumerate(jobs):
            p = j % 2
            if jb[0] == "q":
                e.wait_ge(sem_v, vv0 + 1)
                if j >= 2:
                    e.wait_ge(sem_v, vdone[j - 2])
                RQ.wait(e, p, rq_t[j])
                e.activation(wfc[p][:], wint[p][:], AF.Copy)
                e.activation(sfc[p][:], wsc[p][:], AF.Copy,
                             scale=cst[:, 4:5])
                sinc(e, 2)
    block.scalar(p0_scalar)

    def p0_vector(e):
        e.memset(ones1b[:], 1.0)
        e.memset(ones1f[:], 1.0)
        e.memset(onest[:], 1.0)
        e.memset(cst[:, 0:1], 1e-5)
        e.memset(cst[:, 1:2], 1e-6)
        e.memset(cst[:, 2:3], LN127)
        e.memset(cst[:, 3:4], -0.5)
        e.memset(cst[:, 4:5], SC24)
        vinc(e, 1)
        e.wait_ge(sem_s, ss0 + 1)
        e.tensor_scalar_mul(acol[:], acol[:], -1.0)
        for j, jb in enumerate(jobs):
            p = j % 2
            if j >= 2:
                RBS.wait(e, p, rbs_t[j - 2])
            if jb[0] == "f":
                RF.wait(e, p, rf_t[j])
                e.tensor_copy(bout[p][:, :jb[3]], fin[p][:, :jb[3]])
                vinc(e, 1)
            else:
                e.wait_ge(sem_s, ss_at[j])
                e.tensor_mul(wfc[p][:], wfc[p][:], sfc[p][:])
                e.tensor_copy(bout[p][:, :2048], wfc[p][:])
                vinc(e, 1)
        e.wait_ge(sem_ld, base_small)
        e.tensor_copy(biasrow[:], fbrow[:])
        vinc(e, 1)
    block.vector(p0_vector)

    def p0_store(e):
        for j, jb in enumerate(jobs):
            p = j % 2
            e.wait_ge(sem_v, vdone[j])
            if jb[0] == "f":
                RBS.load(e, p, jb[2], bout[p][:, :jb[3]])
            else:
                r0, c0 = jb[1], jb[2]
                RBS.load(e, p,
                         wdq_bf[c0 // 512:c0 // 512 + 4, :, r0 // 128, :
                                ].transpose([1, 0, 2]),
                         bout[p][:, :2048])
    block.gpsimd(p0_store)

    barrier()
    p0.close()

    def emit_ln_quant(src, tch, quant, actT, tag, dbg_hist=False):
        dbg_lnt = None
        if dbg_hist:
            dbg_lnt = nc.dram_tensor(f"dbg_lnt_{tag}", [len(tch), 128, H], F32,
                                     kind="ExternalOutput").ap()
        sc = ExitStack()
        xt = [sc.enter_context(nc.sbuf_tensor(f"{tag}xt{i}", [128, H], F32))
              for i in range(2)]
        sqt = sc.enter_context(nc.sbuf_tensor(f"{tag}sq", [128, H], F32))
        lnt = sc.enter_context(nc.sbuf_tensor(f"{tag}ln", [128, H], F32))
        stat = sc.enter_context(nc.sbuf_tensor(f"{tag}st", [128, 20], F32))
        hvar = sc.enter_context(nc.sbuf_tensor(f"{tag}hv", [128, 1], F32))
        hrstd = sc.enter_context(nc.sbuf_tensor(f"{tag}hr", [128, 1], F32))
        hrm = sc.enter_context(nc.sbuf_tensor(f"{tag}hm", [128, 1], F32))
        hinv = sc.enter_context(nc.sbuf_tensor(f"{tag}hi", [128, 1], F32))
        shist = sc.enter_context(nc.sbuf_tensor(f"{tag}sh", [128, len(tch), 24],
                                                F32))
        ptq = [sc.enter_context(nc.psum_tensor(f"{tag}pt{i}", [128, 128], F32))
               for i in range(2)]
        NC = len(tch)
        RX = Ring(6, 2)
        RLD = Ring(8, 1)
        xt_t = []
        _c = [RX.cs[i].n for i in range(2)]
        for ci in range(NC):
            _c[ci % 2] += 16
            xt_t.append(_c[ci % 2])
        for i in range(2):
            RX.cs[i].n = _c[i]
        ld0, vv0, ss0, tt0 = LD.n, VV.n, SS.n, TT.n
        vpc = 3 if quant else 2
        base_s = 3 if quant else 2
        spc = base_s + 32
        VV.n = vv0 + vpc * NC
        SS.n = ss0 + spc * NC
        TT.n = tt0 + 32 * NC

        rld0 = RLD.cs[0].n
        if dbg_lnt is not None:
            RLD.cs[0].n += 16 * len(tch)

        def evac_done(gidx):
            ci, c = divmod(gidx, 32)
            return ss0 + spc * ci + base_s + c + 1

        def s_sync(e):
            for ci, (t0, ts_) in enumerate(tch):
                if ci >= 2:
                    e.wait_ge(sem_v, vv0 + vpc * (ci - 1))
                e.dma_start(out=xt[ci % 2][0:ts_, :], in_=src[t0:t0 + ts_, :]
                            ).then_inc(RX.cs[ci % 2].h, 16)
        block.sync(s_sync)

        def s_scalar(e):
            for ci, (t0, ts_) in enumerate(tch):
                p = ci % 2
                sb = ss0 + spc * ci
                vb = vv0 + vpc * ci
                RX.wait(e, ci % 2, xt_t[ci])
                if ci >= 1:
                    e.wait_ge(sem_v, vv0 + vpc * (ci - 1) + 1)
                e.activation(sqt[0:ts_, :], xt[p][0:ts_, :], AF.Square
                             )
                sinc(e, 1)
                e.wait_ge(sem_v, vb + 1)
                e.activation(stat[0:ts_, 6:7], hvar[0:ts_, :], AF.Ln,
                             bias=cst[0:ts_, 0:1])
                e.activation(hrstd[0:ts_, :], stat[0:ts_, 6:7], AF.Exp,
                             scale=cst[0:ts_, 3:4])
                sinc(e, 1)
                if quant:
                    e.wait_ge(sem_v, vb + 2)
                    e.activation(stat[0:ts_, 12:13], hrm[0:ts_, :], AF.Ln)
                    e.activation(hinv[0:ts_, :], stat[0:ts_, 12:13], AF.Exp,
                                 scale=-1.0, bias=cst[0:ts_, 2:3])
                    sinc(e, 1)
                for c in range(32):
                    e.wait_ge(sem_t, tt0 + 32 * ci + c + 1)
                    e.activation(actT[:, c, t0:t0 + ts_], ptq[c % 2][:, 0:ts_],
                                 AF.Copy)
                    sinc(e, 1)
        block.scalar(s_scalar)

        def s_vector(e):
            for ci, (t0, ts_) in enumerate(tch):
                p = ci % 2
                sb = ss0 + spc * ci
                RX.wait(e, ci % 2, xt_t[ci])
                e.reduce_sum(stat[0:ts_, 0:1], xt[p][0:ts_, :], axis=AXX, opt_input=False, opt_output=False)
                e.wait_ge(sem_s, sb + 1)
                e.reduce_sum(stat[0:ts_, 1:2], sqt[0:ts_, :], axis=AXX, opt_input=False, opt_output=False)
                e.drain()
                e.tensor_scalar_mul(stat[0:ts_, 2:3], stat[0:ts_, 0:1], 1.0 / H)
                e.tensor_scalar_mul(stat[0:ts_, 3:4], stat[0:ts_, 1:2], 1.0 / H)
                vwait(e)
                e.tensor_mul(stat[0:ts_, 4:5], stat[0:ts_, 2:3], stat[0:ts_, 2:3])
                vwait(e)
                e.tensor_sub(hvar[0:ts_, :], stat[0:ts_, 3:4], stat[0:ts_, 4:5])
                vinc(e, 1)
                e.wait_ge(sem_s, sb + 2)
                e.tensor_scalar(stat[0:ts_, 9:10], hvar[0:ts_, :], 1.0, 1e-5,
                                ALU.mult, ALU.add)
                e.tensor_mul(stat[0:ts_, 8:9], hrstd[0:ts_, :],
                             hrstd[0:ts_, :])
                vwait(e)
                e.tensor_mul(stat[0:ts_, 11:12], stat[0:ts_, 8:9],
                             stat[0:ts_, 9:10])
                vwait(e)
                e.tensor_scalar(stat[0:ts_, 13:14], stat[0:ts_, 11:12],
                                -0.5, 1.5, ALU.mult, ALU.add)
                vwait(e)
                e.tensor_mul(stat[0:ts_, 15:16], hrstd[0:ts_, :],
                             stat[0:ts_, 13:14])
                vwait(e)
                e.tensor_mul(stat[0:ts_, 17:18], stat[0:ts_, 2:3],
                             stat[0:ts_, 15:16])
                vwait(e)
                e.tensor_scalar_mul(stat[0:ts_, 18:19], stat[0:ts_, 17:18], -1.0)
                if ci >= 1:
                    e.wait_ge(sem_t, tt0 + 32 * ci)
                if dbg_lnt is not None and ci >= 1:
                    e.wait_ge(RLD.cs[0].h, rld0 + 16 * ci)
                vwait(e)
                e.tensor_scalar(lnt[0:ts_, :], xt[p][0:ts_, :],
                                stat[0:ts_, 15:16], stat[0:ts_, 18:19],
                                ALU.mult, ALU.add)
                if quant:
                    vwait(e)
                    e.reduce_max(stat[0:ts_, 10:11], lnt[0:ts_, :], axis=AXX,
                                 apply_absolute_value=True,
                                 opt_input=False, opt_output=False)
                    e.drain()
                    e.tensor_scalar_max(hrm[0:ts_, :], stat[0:ts_, 10:11],
                                        1e-8)
                    vinc(e, 1)
                    e.tensor_scalar_mul(stat[0:ts_, 14:15], hrm[0:ts_, :],
                                        1.0 / 127.0)
                    e.wait_ge(sem_s, sb + 3)
                    vwait(e)
                    e.tensor_mul(stat[0:ts_, 8:9], hinv[0:ts_, :],
                                 stat[0:ts_, 14:15])
                    vwait(e)
                    e.tensor_scalar(stat[0:ts_, 11:12], stat[0:ts_, 8:9], -1.0,
                                    2.0, ALU.mult, ALU.add)
                    vwait(e)
                    e.tensor_mul(stat[0:ts_, 16:17], hinv[0:ts_, :],
                                 stat[0:ts_, 11:12])
                    vwait(e)
                    e.tensor_scalar_mul(lnt[0:ts_, :], lnt[0:ts_, :],
                                        stat[0:ts_, 16:17])
                    e.tensor_scalar(lnt[0:ts_, :], lnt[0:ts_, :], MAGIC, -MAGIC,
                                    ALU.add, ALU.add)
                    e.tensor_scalar(lnt[0:ts_, :], lnt[0:ts_, :], 127.0, -127.0,
                                    ALU.min, ALU.max)
                    e.tensor_scalar_mul(lnt[0:ts_, :], lnt[0:ts_, :],
                                        stat[0:ts_, 14:15])
                    e.tensor_copy(shist[:, ci, 0:1], hvar[:, :])
                    e.tensor_copy(shist[:, ci, 1:2], hrstd[:, :])
                    e.tensor_copy(shist[:, ci, 2:3], hrm[:, :])
                    e.tensor_copy(shist[:, ci, 3:4], hinv[:, :])
                    e.tensor_copy(shist[:, ci, 4:20], stat[:, 4:20])
                    e.tensor_copy(shist[:, ci, 20:24], stat[:, 0:4])
                    vinc(e, 1)
                else:
                    e.tensor_copy(shist[:, ci, 0:1], hvar[:, :])
                    e.tensor_copy(shist[:, ci, 1:2], hrstd[:, :])
                    e.tensor_copy(shist[:, ci, 4:20], stat[:, 4:20])
                    e.tensor_copy(shist[:, ci, 20:24], stat[:, 0:4])
                    vinc(e, 1)
        block.vector(s_vector)

        if dbg_lnt is not None:
            def s_lntdump(e):
                for ci in range(len(tch)):
                    e.wait_ge(sem_v, vv0 + vpc * (ci + 1))
                    e.dma_start(out=dbg_lnt[ci], in_=lnt[:]
                                ).then_inc(RLD.cs[0].h, 16)
            block.gpsimd(s_lntdump)

        def s_tensor(e):
            for ci, (t0, ts_) in enumerate(tch):
                e.wait_ge(sem_v, vv0 + vpc * (ci + 1))
                for c in range(32):
                    gidx = 32 * ci + c
                    if gidx >= 2:
                        e.wait_ge(sem_s, evac_done(gidx - 2))
                    e.transpose(ptq[c % 2][:, 0:ts_],
                                lnt[0:ts_, c * 128:(c + 1) * 128],
                                identf[0:ts_, 0:ts_]).then_inc(sem_t, 1)
        block.tensor(s_tensor)

        if dbg_hist:
            dbg_h = nc.dram_tensor(f"dbg_hist_{tag}", [128, len(tch), 24], F32,
                                   kind="ExternalOutput").ap()
            hv = VV.n

            def s_hdump(e):
                e.wait_ge(sem_v, hv)
                e.dma_start(out=dbg_h, in_=shist[:]).then_inc(sem_st, 16)
            ST.n += 16
            block.gpsimd(s_hdump)

        barrier()
        sc.close()

    def emit_ffn(res_src, dst, tch, actT, tag):
        sc = ExitStack()
        ngrp1 = (len(tch) + 1) // 2
        grps = [tch[:ngrp1], tch[ngrp1:]]
        nps = max(len(g) for g in grps)
        wslab = [sc.enter_context(nc.sbuf_tensor(f"{tag}ws{i}", [128, 32, 512],
                                                 BF16)) for i in range(2)]
        xres = [sc.enter_context(nc.sbuf_tensor(f"{tag}xr{i}", [128, 512], F32))
                for i in range(nps)]
        otile = [sc.enter_context(nc.sbuf_tensor(f"{tag}ot{i}", [128, 512], F32))
                 for i in range(nps)]
        psq = [sc.enter_context(nc.psum_tensor(f"{tag}ps{i}", [128, 512], F32))
               for i in range(nps)]
        glist = []
        for og in range(8):
            for g in range(2):
                glist.append((og, grps[g]))
        NG = len(glist)
        RW = Ring(0, 2)
        RXR = Ring(4, nps)
        ROT = Ring(9, nps)
        vv0, tt0 = VV.n, TT.n
        tt_at, vv_at = [], []
        wm_t, xr_t, ot_t = [], [], []
        cw = [RW.cs[i].n for i in range(2)]
        cx = [RXR.cs[i].n for i in range(nps)]
        co = [ROT.cs[i].n for i in range(nps)]
        bb, cc = tt0, vv0
        for og in range(8):
            cw[og % 2] += 16
            wm_t.append(cw[og % 2])
        for og, tl in glist:
            L = len(tl)
            xrow = []
            orow = []
            for k in range(L):
                cx[k] += 16
                xrow.append(cx[k])
                co[k] += 16
                orow.append(co[k])
            xr_t.append(xrow)
            ot_t.append(orow)
            bb += 1 + L
            cc += L
            tt_at.append(bb)
            vv_at.append(cc)
        TT.n, VV.n = bb, cc
        for i in range(2):
            RW.cs[i].n = cw[i]
        for k in range(nps):
            RXR.cs[k].n = cx[k]
            ROT.cs[k].n = co[k]

        def s_sync(e):
            for gi, (og, tl) in enumerate(glist):
                if gi % 2 == 0:
                    if og >= 2:
                        e.wait_ge(sem_t, tt_at[2 * (og - 2) + 1])
                    e.dma_start(out=wslab[og % 2][:],
                                in_=wdq_bf[og]).then_inc(RW.cs[og % 2].h, 16)
                if gi >= 1:
                    e.wait_ge(sem_v, vv_at[gi - 1])
                for k, (t0, ts_) in enumerate(tl):
                    e.dma_start(out=xres[k][0:ts_, :],
                                in_=res_src[t0:t0 + ts_,
                                            og * 512:(og + 1) * 512]
                                ).then_inc(RXR.cs[k].h, 16)
        block.sync(s_sync)

        def s_tensor(e):
            for gi, (og, tl) in enumerate(glist):
                if gi >= 1:
                    e.wait_ge(sem_v, vv_at[gi - 1])
                e.wait_ge(RW.cs[og % 2].h, wm_t[og])
                ins = None
                for i in range(32):
                    for k, (t0, ts_) in enumerate(tl):
                        ins = e.matmul(psq[k][0:ts_, :],
                                       actT[:, i, t0:t0 + ts_],
                                       wslab[og % 2][:, i, :],
                                       start=(i == 0), stop=False)
                ins.then_inc(sem_t, 1)
                for k, (t0, ts_) in enumerate(tl):
                    e.matmul(psq[k][0:ts_, :], ones1b[0:1, 0:ts_],
                             biasrow[0:1, og * 512:(og + 1) * 512],
                             start=False, stop=True).then_inc(sem_t, 1)
        block.tensor(s_tensor)

        def s_vector(e):
            for gi, (og, tl) in enumerate(glist):
                base_t = (tt0 if gi == 0 else tt_at[gi - 1]) + 1
                for k, (t0, ts_) in enumerate(tl):
                    e.wait_ge(sem_t, base_t + k + 1)
                    e.wait_ge(RXR.cs[k].h, xr_t[gi][k])
                    if gi >= 2:
                        e.wait_ge(ROT.cs[k].h, ot_t[gi - 2][k])
                    e.scalar_tensor_tensor(otile[k][0:ts_, :], psq[k][0:ts_, :],
                                           0.5, xres[k][0:ts_, :],
                                           ALU.mult, ALU.add)
                    vinc(e, 1)
        block.vector(s_vector)

        def s_store(e):
            for gi, (og, tl) in enumerate(glist):
                base_v = vv0 if gi == 0 else vv_at[gi - 1]
                for k, (t0, ts_) in enumerate(tl):
                    e.wait_ge(sem_v, base_v + k + 1)
                    e.dma_start(out=dst[t0:t0 + ts_, og * 512:(og + 1) * 512],
                                in_=otile[k][0:ts_, :]).then_inc(ROT.cs[k].h, 16)
        block.gpsimd(s_store)

        barrier()
        sc.close()

    scA = ExitStack()
    actT = scA.enter_context(nc.sbuf_tensor("actT", [128, 32, TF], BF16))
    emit_ln_quant(x_in, TCH_F, not DBG_NOQUANT, actT, "q1", dbg_hist=debug)
    if debug:
        dbg_act1 = nc.dram_tensor("dbg_act1", [128, 32 * TF], BF16,
                                  kind="ExternalOutput").ap()

        def dbg_a1(e):
            e.dma_start(out=dbg_act1,
                        in_=actT[:].rearrange("p a b -> p (a b)")
                        ).then_inc(sem_st, 16)
        ST.n += 16
        block.gpsimd(dbg_a1)
    if stop_after == "q1":
        if debug:
            fin_st0 = ST.n

            def fin_gp0(e):
                e.wait_ge(sem_st, fin_st0)
            block.gpsimd(fin_gp0)
        blk_cm.__exit__(None, None, None)
        scA.close()
        pers.close()
        return nc
    emit_ffn(x_in, u1_scr, TCH_F, actT, "f1")
    emit_ln_quant(u1_scr, TCH_F, False, actT, "q2")

    scC = ExitStack()
    wstat = [scC.enter_context(nc.sbuf_tensor(f"pw{i}", [128, 32, 128], BF16))
             for i in range(2)]
    xst = [scC.enter_context(nc.sbuf_tensor(f"px{i}", [128, TF], F32))
           for i in range(2)]
    xct = [scC.enter_context(nc.sbuf_tensor(f"pc{i}", [128, TLOC], F32))
           for i in range(2)]
    utf = [scC.enter_context(nc.sbuf_tensor(f"pu{i}", [128, TLOC], F32))
           for i in range(2)]
    utb = [scC.enter_context(nc.sbuf_tensor(f"pub{i}", [128, TLOC], BF16))
           for i in range(2)]
    gtile = [scC.enter_context(nc.sbuf_tensor(f"pg{i}", [128, TLOC], F32))
             for i in range(2)]
    psp = [scC.enter_context(nc.psum_tensor(f"pp{i}", [128, 512], F32))
           for i in range(6)]

    RWS = Ring(0, 2)
    RUF = Ring(2, 2)
    RUB = Ring(4, 2)
    RGT = Ring(6, 2)
    vv0, ss0, tt0 = VV.n, SS.n, TT.n
    ws_t = []
    cws = [RWS.cs[i].n for i in range(2)]
    for oc in range(128):
        cws[oc % 2] += 16
        ws_t.append(cws[oc % 2])
    for i in range(2):
        RWS.cs[i].n = cws[i]
    uf_t, ub_t, gt_t = {}, {}, {}
    cuf = [RUF.cs[i].n for i in range(2)]
    cub = [RUB.cs[i].n for i in range(2)]
    cgt = [RGT.cs[i].n for i in range(2)]
    for oc in range(128):
        p = oc % 2
        if oc < 64:
            cuf[p] += 16
            uf_t[oc] = cuf[p]
            cub[p] += 16
            ub_t[oc] = cub[p]
        else:
            cgt[p] += 16
            gt_t[oc] = cgt[p]
    for i in range(2):
        RUF.cs[i].n = cuf[i]
        RUB.cs[i].n = cub[i]
        RGT.cs[i].n = cgt[i]
    ss_at, vv_at = [], []
    c, d = ss0, vv0
    for oc in range(128):
        c += 4 if oc < 64 else 3
        d += 2 if oc < 64 else 0
        ss_at.append(c)
        vv_at.append(d)
    SS.n, VV.n = c, d
    TT.n = tt0 + 128

    def c_sync(e):
        for oc in range(128):
            if oc >= 2:
                e.wait_ge(sem_t, tt0 + oc - 1)
            e.dma_start(out=wstat[oc % 2][:],
                        in_=win_bf[oc]).then_inc(RWS.cs[oc % 2].h, 16)
    block.sync(c_sync)

    def c_tensor(e):
        for oc in range(128):
            pp = (oc % 2) * 3
            if oc >= 2:
                e.wait_ge(sem_s, ss_at[oc - 2] - (3 if oc - 2 < 64 else 2))
            RWS.wait(e, oc % 2, ws_t[oc])
            ins = None
            for i in range(32):
                for k, (f0, fs) in enumerate(TFREE_F):
                    ins = e.matmul(psp[pp + k][:, 0:fs],
                                   wstat[oc % 2][:, i, :],
                                   actT[:, i, f0:f0 + fs],
                                   start=(i == 0), stop=(i == 31))
            ins.then_inc(sem_t, 1)
    block.tensor(c_tensor)

    def c_scalar(e):
        for oc in range(128):
            pp = (oc % 2) * 3
            vb = vv0 if oc == 0 else vv_at[oc - 1]
            e.wait_ge(sem_t, tt0 + oc + 1)
            if oc < 64:
                if oc >= 2:
                    e.wait_ge(sem_v, vv_at[oc - 2] - 1)
                for k, (f0, fs) in enumerate(TFREE_F):
                    ins = e.activation(xst[oc % 2][:, f0:f0 + fs],
                                       psp[pp + k][:, 0:fs], AF.Copy)
                sinc(e, 1)
                e.wait_ge(sem_v, vb + 1)
                if oc >= 2:
                    RUF.wait(e, oc % 2, uf_t[oc - 2])
                e.activation(utf[oc % 2][:], xct[oc % 2][:], AF.Silu
                             )
                sinc(e, 3)
            else:
                if oc >= 66:
                    RGT.wait(e, oc % 2, gt_t[oc - 2])
                e.activation(gtile[oc % 2][:, 0:509], psp[pp][:, 3:512], AF.Copy)
                e.activation(gtile[oc % 2][:, 509:1021], psp[pp + 1][:, 0:512],
                             AF.Copy)
                e.activation(gtile[oc % 2][:, 1021:1024], psp[pp + 2][:, 0:3],
                             AF.Copy)
                sinc(e, 3)
    block.scalar(c_scalar)

    def c_vector(e):
        for oc in range(64):
            sb = ss0 if oc == 0 else ss_at[oc - 1]
            e.wait_ge(sem_s, sb + 1)
            dg = oc
            x = xst[oc % 2]
            e.tensor_scalar_mul(x[:, 0:HALO], x[:, 0:HALO], maskc[:, 0:1])
            e.tensor_scalar(xct[oc % 2][:], x[:, 0:TLOC],
                            convwc[:, dg, 0:1], convbc[:, dg:dg + 1],
                            ALU.mult, ALU.add)
            for j in (1, 2, 3):
                ins = e.scalar_tensor_tensor(xct[oc % 2][:], x[:, j:j + TLOC],
                                             convwc[:, dg, j:j + 1],
                                             xct[oc % 2][:],
                                             ALU.mult, ALU.add)
            vinc(e, 1)
            e.wait_ge(sem_s, ss_at[oc])
            if oc >= 2:
                RUB.wait(e, oc % 2, ub_t[oc - 2])
            e.tensor_copy(utb[oc % 2][:], utf[oc % 2][:])
            vinc(e, 1)
    block.vector(c_vector)

    def c_store(e):
        for oc in range(128):
            dg = oc if oc < 64 else oc - 64
            if oc < 64:
                e.wait_ge(sem_s, ss_at[oc])
                e.dma_start(out=u_scr[dg * 128:(dg + 1) * 128, :],
                            in_=utf[oc % 2][:]).then_inc(RUF.cs[oc % 2].h, 16)
                e.wait_ge(sem_v, vv_at[oc])
                e.dma_start(out=ubf_scr[dg * 128:(dg + 1) * 128, :],
                            in_=utb[oc % 2][:]).then_inc(RUB.cs[oc % 2].h, 16)
            else:
                e.wait_ge(sem_s, ss_at[oc])
                e.dma_start(out=gate_scr[dg * 128:(dg + 1) * 128, :],
                            in_=gtile[oc % 2][:]).then_inc(RGT.cs[oc % 2].h, 16)
    block.gpsimd(c_store)

    barrier()
    scC.close()
    scA.close()

    scD = ExitStack()
    btall = scD.enter_context(nc.sbuf_tensor("btall", [128, NST, TLOC], BF16))
    ctall = scD.enter_context(nc.sbuf_tensor("ctall", [128, NST, TLOC], BF16))

    scD1 = ExitStack()
    wxt = [scD1.enter_context(nc.sbuf_tensor(f"dw{i}", [128, 288], BF16))
           for i in range(2)]
    ubt = [scD1.enter_context(nc.sbuf_tensor(f"du{i}", [128, TLOC], BF16))
           for i in range(2)]
    ssmAB = scD1.enter_context(nc.sbuf_tensor("ssmAB", [128, 2, TLOC], F32))
    ssmC = scD1.enter_context(nc.sbuf_tensor("ssmC", [32, TLOC], F32))
    ssmtok = scD1.enter_context(nc.sbuf_tensor("ssmtok", [128, 8, 288], F32))
    stat2 = scD1.enter_context(nc.sbuf_tensor("stat2", [128, 8], F32))
    sqsm = scD1.enter_context(nc.sbuf_tensor("sqsm", [128, R], F32))
    browf = scD1.enter_context(nc.sbuf_tensor("browf", [16, TLOC], BF16))
    crowf = scD1.enter_context(nc.sbuf_tensor("crowf", [16, TLOC], BF16))
    tsnT = scD1.enter_context(nc.sbuf_tensor("tsnT", [128, 2, TLOC], BF16))
    psx = [scD1.enter_context(nc.psum_tensor(f"dps{i}", [128, 512], F32))
           for i in range(6)]
    ptr2 = [scD1.enter_context(nc.psum_tensor(f"dpt{i}", [128, 128], F32))
            for i in range(2)]

    RWX = Ring(0, 2)
    RUBT = Ring(2, 2)
    tt0 = TT.n
    TT.n += 64
    wx_t, ub_t2 = [], []
    _a = [RWX.cs[i].n for i in range(2)]
    _b = [RUBT.cs[i].n for i in range(2)]
    for ic in range(64):
        _a[ic % 2] += 16
        _b[ic % 2] += 16
        wx_t.append(_a[ic % 2])
        ub_t2.append(_b[ic % 2])
    for i in range(2):
        RWX.cs[i].n = _a[i]
        RUBT.cs[i].n = _b[i]

    def d_sync(e):
        for ic in range(64):
            if ic >= 2:
                e.wait_ge(sem_t, tt0 + ic - 1)
            e.dma_start(out=wxt[ic % 2][:],
                        in_=wx_bf[ic * 128:(ic + 1) * 128, :]
                        ).then_inc(RWX.cs[ic % 2].h, 16)
            e.dma_start(out=ubt[ic % 2][:],
                        in_=ubf_scr[ic * 128:(ic + 1) * 128, :]
                        ).then_inc(RUBT.cs[ic % 2].h, 16)
    block.sync(d_sync)

    def d_tensor(e):
        for ic in range(64):
            e.wait_ge(RWX.cs[ic % 2].h, wx_t[ic])
            e.wait_ge(RUBT.cs[ic % 2].h, ub_t2[ic])
            ins = None
            for m, (m0, ms) in enumerate([(0, 128), (128, 128), (256, 32)]):
                for tf, (f0, fs) in enumerate([(0, 512), (512, 512)]):
                    ins = e.matmul(psx[m * 2 + tf][0:ms, :],
                                   wxt[ic % 2][:, m0:m0 + ms],
                                   ubt[ic % 2][:, f0:f0 + fs],
                                   start=(ic == 0), stop=(ic == 63))
            ins.then_inc(sem_t, 1)
    block.tensor(d_tensor)
    t_ssm_done = TT.n

    s_ssm_ev = SS.bump(1)

    def d_scalar1(e):
        e.wait_ge(sem_t, t_ssm_done)
        for tf, (f0, fs) in enumerate([(0, 512), (512, 512)]):
            e.activation(ssmAB[:, 0, f0:f0 + fs], psx[0 + tf][:, 0:fs], AF.Copy)
            e.activation(ssmAB[:, 1, f0:f0 + fs], psx[2 + tf][:, 0:fs], AF.Copy)
            ins = e.activation(ssmC[0:32, f0:f0 + fs], psx[4 + tf][0:32, 0:fs],
                               AF.Copy)
        sinc(e, 1)
    block.scalar(d_scalar1)

    tt_tr0 = TT.n
    TT.n += 24
    ss_tr0 = SS.n
    SS.n += 24

    def d_tensor2(e):
        e.wait_ge(sem_s, s_ssm_ev)
        k = 0
        for tt_i in range(8):
            for part in range(2):
                k += 1
                if k >= 3:
                    e.wait_ge(sem_s, ss_tr0 + k - 2)
                e.transpose(ptr2[k % 2][:, 0:128],
                            ssmAB[:, part, tt_i * 128:(tt_i + 1) * 128],
                            identf[:, :]).then_inc(sem_t, 1)
            k += 1
            if k >= 3:
                e.wait_ge(sem_s, ss_tr0 + k - 2)
            e.transpose(ptr2[k % 2][:, 0:32],
                        ssmC[0:32, tt_i * 128:(tt_i + 1) * 128],
                        identf[0:32, 0:32]).then_inc(sem_t, 1)
    block.tensor(d_tensor2)

    def d_scalar2(e):
        k = 0
        for tt_i in range(8):
            for part in range(2):
                k += 1
                e.wait_ge(sem_t, tt_tr0 + k)
                e.activation(ssmtok[:, tt_i, part * 128:(part + 1) * 128],
                             ptr2[k % 2][:, 0:128], AF.Copy)
                sinc(e, 1)
            k += 1
            e.wait_ge(sem_t, tt_tr0 + k)
            e.activation(ssmtok[:, tt_i, 256:288], ptr2[k % 2][:, 0:32],
                         AF.Copy)
            sinc(e, 1)
    block.scalar(d_scalar2)

    groups = [(0, R, 1.0 / R), (R, NST, 1.0 / NST), (R + NST, NST, 1.0 / NST)]
    vv0b, ss0b = VV.n, SS.n
    VV.n += 8 * 3 * 2
    SS.n += 8 * 3 * 2

    def d_scalar3(e):
        k = 0
        for tt_i in range(8):
            for (c0, cw, sc_) in groups:
                k += 1
                e.wait_ge(sem_s, ss_tr0 + 3 * (tt_i + 1))
                if k >= 2:
                    e.wait_ge(sem_v, vv0b + 2 * (k - 1) - 1)
                e.activation(sqsm[:, 0:cw], ssmtok[:, tt_i, c0:c0 + cw],
                             AF.Square)
                sinc(e, 1)
                e.wait_ge(sem_v, vv0b + 2 * (k - 1) + 1)
                e.activation(stat2[:, 1:2], stat2[:, 3:4], AF.Ln,
                             bias=cst[:, 1:2])
                e.activation(stat2[:, 2:3], stat2[:, 1:2], AF.Exp,
                             scale=cst[:, 3:4])
                sinc(e, 1)
    block.scalar(d_scalar3)

    def d_vector2(e):
        k = 0
        for tt_i in range(8):
            for (c0, cw, sc_) in groups:
                k += 1
                e.wait_ge(sem_s, ss0b + 2 * (k - 1) + 1)
                e.reduce_sum(stat2[:, 0:1], sqsm[:, 0:cw], axis=AXX, opt_input=False, opt_output=False)
                e.drain()
                e.tensor_scalar_mul(stat2[:, 3:4], stat2[:, 0:1], sc_)
                vinc(e, 1)
                e.wait_ge(sem_s, ss0b + 2 * k)
                vwait(e)
                e.tensor_scalar_mul(ssmtok[:, tt_i, c0:c0 + cw],
                                    ssmtok[:, tt_i, c0:c0 + cw],
                                    stat2[:, 2:3])
                vinc(e, 1)
    block.vector(d_vector2)
    vv_rms_end = VV.n

    tt0c = TT.n
    TT.n += 32
    ss0c = SS.n
    SS.n += 32

    def d_tensor3(e):
        e.wait_ge(sem_v, vv_rms_end)
        k = 0
        for tt_i in range(8):
            for cpart in range(2):
                k += 1
                if k >= 3:
                    e.wait_ge(sem_s, ss0c + k - 2)
                e.transpose(ptr2[k % 2][:, 0:128],
                            ssmtok[:, tt_i, cpart * 128:(cpart + 1) * 128],
                            identf[:, :]).then_inc(sem_t, 1)
        for tt_i in range(8):
            k += 1
            if k >= 3:
                e.wait_ge(sem_s, ss0c + k - 2)
            e.transpose(ptr2[k % 2][0:16, 0:128],
                        ssmtok[:, tt_i, R:R + NST],
                        identf[:, :]).then_inc(sem_t, 1)
        for tt_i in range(8):
            k += 1
            if k >= 3:
                e.wait_ge(sem_s, ss0c + k - 2)
            e.transpose(ptr2[k % 2][0:16, 0:128],
                        ssmtok[:, tt_i, R + NST:R + 2 * NST],
                        identf[:, :]).then_inc(sem_t, 1)
    block.tensor(d_tensor3)

    def d_scalar4(e):
        k = 0
        for tt_i in range(8):
            for cpart in range(2):
                k += 1
                e.wait_ge(sem_t, tt0c + k)
                e.activation(tsnT[:, cpart, tt_i * 128:(tt_i + 1) * 128],
                             ptr2[k % 2][:, 0:128], AF.Copy)
                sinc(e, 1)
        for tt_i in range(8):
            k += 1
            e.wait_ge(sem_t, tt0c + k)
            e.activation(browf[0:16, tt_i * 128:(tt_i + 1) * 128],
                         ptr2[k % 2][0:16, 0:128], AF.Copy)
            sinc(e, 1)
        for tt_i in range(8):
            k += 1
            e.wait_ge(sem_t, tt0c + k)
            e.activation(crowf[0:16, tt_i * 128:(tt_i + 1) * 128],
                         ptr2[k % 2][0:16, 0:128], AF.Copy)
            sinc(e, 1)
    block.scalar(d_scalar4)
    s_dsc4_end = SS.n

    st0bc = ST.n
    ST.n += 16 * 4

    def d_bcast(e):
        e.wait_ge(sem_s, s_dsc4_end)
        e.dma_start(out=brow_d, in_=browf[:]).then_inc(sem_st, 16)
        e.dma_start(out=crow_d, in_=crowf[:]).then_inc(sem_st, 16)
        e.wait_ge(sem_st, st0bc + 32)
        e.dma_start(out=btall[:],
                    in_=bass.AP(brow_d.tensor, brow_d.offset,
                                [[0, 128], [1, NST * TLOC]])
                    ).then_inc(sem_st, 16)
        e.dma_start(out=ctall[:],
                    in_=bass.AP(crow_d.tensor, crow_d.offset,
                                [[0, 128], [1, NST * TLOC]])
                    ).then_inc(sem_st, 16)
    block.gpsimd(d_bcast)

    barrier()

    wdtt = [scD1.enter_context(nc.sbuf_tensor(f"dtw{i}", [128, 128], BF16))
            for i in range(2)]
    rzt = [scD1.enter_context(nc.sbuf_tensor(f"rzt{i}", [128, 512], F32))
           for i in range(2)]
    azt = [scD1.enter_context(nc.sbuf_tensor(f"azt{i}", [128, 512], F32))
           for i in range(2)]
    ezt = [scD1.enter_context(nc.sbuf_tensor(f"ezt{i}", [128, 512], F32))
           for i in range(2)]
    lzt = [scD1.enter_context(nc.sbuf_tensor(f"lzt{i}", [128, 512], F32))
           for i in range(2)]
    dtt_o = [scD1.enter_context(nc.sbuf_tensor(f"dto{i}", [128, TLOC], F32))
             for i in range(2)]

    RWD = Ring(4, 2)
    RDO = Ring(6, 2)
    tt0, ss0, vv0 = TT.n, SS.n, VV.n
    TT.n += 2 * 64
    SS.n += 2 * 64
    VV.n += 2 * 64
    wd_t = []
    _a = [RWD.cs[i].n for i in range(2)]
    for oc in range(64):
        _a[0] += 16
        _a[1] += 16
        wd_t.append((_a[0], _a[1]))
    for i in range(2):
        RWD.cs[i].n = _a[i]
    do_t = []
    _b = [RDO.cs[i].n for i in range(2)]
    for oc in range(64):
        _b[oc % 2] += 16
        do_t.append(_b[oc % 2])
    for i in range(2):
        RDO.cs[i].n = _b[i]

    def e_sync(e):
        for oc in range(64):
            for kc in range(2):
                if oc >= 1 and kc == 0:
                    e.wait_ge(sem_t, tt0 + 2 * oc)
                e.dma_start(out=wdtt[kc][:],
                            in_=wdt_bf[kc * 128:(kc + 1) * 128,
                                       oc * 128:(oc + 1) * 128]
                            ).then_inc(RWD.cs[kc].h, 16)
    block.sync(e_sync)

    def e_tensor(e):
        for oc in range(64):
            pp = (oc % 2) * 2
            if oc >= 2:
                e.wait_ge(sem_s, ss0 + 2 * (oc - 2) + 2)
            e.wait_ge(RWD.cs[0].h, wd_t[oc][0])
            e.wait_ge(RWD.cs[1].h, wd_t[oc][1])
            for tf, (f0, fs) in enumerate([(0, 512), (512, 512)]):
                ins = None
                for kc in range(2):
                    ins = e.matmul(psx[pp + tf][:, 0:fs],
                                   wdtt[kc][:], tsnT[:, kc, f0:f0 + fs],
                                   start=(kc == 0), stop=(kc == 1))
                ins.then_inc(sem_t, 1)
    block.tensor(e_tensor)

    def e_scalar(e):
        for oc in range(64):
            pp = (oc % 2) * 2
            for tf in range(2):
                idx = 2 * oc + tf
                e.wait_ge(sem_t, tt0 + idx + 1)
                if idx >= 2:
                    e.wait_ge(sem_v, vv0 + idx - 1)
                e.activation(rzt[tf][:], psx[pp + tf][:, 0:512], AF.Relu,
                             bias=bdtc[:, oc:oc + 1])
                e.activation(azt[tf][:], psx[pp + tf][:, 0:512], AF.Abs,
                             bias=bdtc[:, oc:oc + 1])
                e.activation(ezt[tf][:], azt[tf][:], AF.Exp, scale=-1.0)
                e.activation(lzt[tf][:], ezt[tf][:], AF.Ln, bias=1.0)
                sinc(e, 1)
    block.scalar(e_scalar)

    def e_vector(e):
        for oc in range(64):
            for tf, (f0, fs) in enumerate([(0, 512), (512, 512)]):
                idx = 2 * oc + tf
                e.wait_ge(sem_s, ss0 + idx + 1)
                if oc >= 2 and tf == 0:
                    RDO.wait(e, oc % 2, do_t[oc - 2])
                e.tensor_add(dtt_o[oc % 2][:, f0:f0 + fs], rzt[tf][:],
                             lzt[tf][:])
                vinc(e, 1)
    block.vector(e_vector)

    def e_store(e):
        for oc in range(64):
            e.wait_ge(sem_v, vv0 + 2 * oc + 2)
            e.dma_start(out=dt_scr[oc * 128:(oc + 1) * 128, :],
                        in_=dtt_o[oc % 2][:]).then_inc(RDO.cs[oc % 2].h, 16)
    block.gpsimd(e_store)

    barrier()
    scD1.close()

    scD2 = ExitStack()
    dtt = [scD2.enter_context(nc.sbuf_tensor(f"sdt{i}", [128, TLOC], F32))
           for i in range(2)]
    ut = [scD2.enter_context(nc.sbuf_tensor(f"sut{i}", [128, TLOC], F32))
          for i in range(2)]
    gt_ = scD2.enter_context(nc.sbuf_tensor("sgt", [128, TLOC], F32))
    dAt = [scD2.enter_context(nc.sbuf_tensor(f"sda{i}", [128, TLOC], F32))
           for i in range(2)]
    dBu = scD2.enter_context(nc.sbuf_tensor("sdbu", [128, TLOC], F32))
    Sti = scD2.enter_context(nc.sbuf_tensor("ssti", [128, TLOC], F32))
    tmpt = scD2.enter_context(nc.sbuf_tensor("stmp", [128, TLOC], F32))
    y0t = [scD2.enter_context(nc.sbuf_tensor(f"sy0{i}", [128, TLOC], F32))
           for i in range(2)]
    Pt = [scD2.enter_context(nc.sbuf_tensor(f"sP{i}", [128, TLOC], F32))
          for i in range(2)]

    RDT = Ring(0, 2)
    RUT = Ring(2, 2)
    RPS = Ring(4, 2)
    RYS = Ring(6, 2)
    vv0, ss0 = VV.n, SS.n
    SS.n += 16 * 64
    VV.n += 18 * 64
    dt_t, ut_t, ps_t, ys_t = [], [], [], []
    _a = [RDT.cs[i].n for i in range(2)]
    _b = [RUT.cs[i].n for i in range(2)]
    _c = [RPS.cs[i].n for i in range(2)]
    _d = [RYS.cs[i].n for i in range(2)]
    for dg in range(64):
        p = dg % 2
        _a[p] += 16; dt_t.append(_a[p])
        _b[p] += 16; ut_t.append(_b[p])
        _c[p] += 16; ps_t.append(_c[p])
        _d[p] += 16; ys_t.append(_d[p])
    for i in range(2):
        RDT.cs[i].n = _a[i]; RUT.cs[i].n = _b[i]
        RPS.cs[i].n = _c[i]; RYS.cs[i].n = _d[i]

    def f_sync(e):
        for dg in range(64):
            if dg >= 2:
                e.wait_ge(sem_v, vv0 + 18 * (dg - 1))
            e.dma_start(out=dtt[dg % 2][:],
                        in_=dt_scr[dg * 128:(dg + 1) * 128, :]
                        ).then_inc(RDT.cs[dg % 2].h, 16)
            e.dma_start(out=ut[dg % 2][:],
                        in_=u_scr[dg * 128:(dg + 1) * 128, :]
                        ).then_inc(RUT.cs[dg % 2].h, 16)
    block.sync(f_sync)

    def f_scalar(e):
        for dg in range(64):
            e.wait_ge(RDT.cs[dg % 2].h, dt_t[dg])
            for n in range(NST):
                idx = dg * NST + n
                if idx >= 2:
                    dgp, np_ = divmod(idx - 2, NST)
                    e.wait_ge(sem_v, vv0 + 18 * dgp + 1 + np_ + 1)
                e.activation(dAt[n % 2][:], dtt[dg % 2][:], AF.Exp,
                             scale=acol[:, idx:idx + 1])
                sinc(e, 1)
    block.scalar(f_scalar)

    def f_vector(e):
        for dg in range(64):
            e.wait_ge(RDT.cs[dg % 2].h, dt_t[dg])
            e.wait_ge(RUT.cs[dg % 2].h, ut_t[dg])
            if dg >= 2:
                RPS.wait(e, dg % 2, ps_t[dg - 2])
                RYS.wait(e, dg % 2, ys_t[dg - 2])
            e.tensor_tensor_scan(Pt[dg % 2][:], onest[:], dtt[dg % 2][:], 0.0,
                                 ALU.mult, ALU.add)
            e.tensor_mul(gt_[:], dtt[dg % 2][:], ut[dg % 2][:])
            e.tensor_copy(ptotc[:, dg:dg + 1], Pt[dg % 2][:, TLOC - 1:TLOC]
                          )
            vinc(e, 1)
            for n in range(NST):
                e.wait_ge(sem_s, ss0 + 16 * dg + n + 1)
                e.tensor_mul(dBu[:], gt_[:], btall[:, n, :])
                e.tensor_tensor_scan(Sti[:], dAt[n % 2][:], dBu[:], 0.0,
                                     ALU.mult, ALU.add)
                vinc(e, 1)
                e.tensor_mul(tmpt[:], Sti[:], ctall[:, n, :])
                e.tensor_copy(zall[:, dg * NST + n:dg * NST + n + 1],
                              Sti[:, TLOC - 1:TLOC])
                if n == 0:
                    ins = e.tensor_copy(y0t[dg % 2][:], tmpt[:])
                else:
                    ins = e.tensor_add(y0t[dg % 2][:], y0t[dg % 2][:], tmpt[:])
            vinc(e, 1)
    block.vector(f_vector)

    def f_store(e):
        for dg in range(64):
            e.wait_ge(sem_v, vv0 + 18 * (dg + 1))
            e.dma_start(out=p_scr[dg * 128:(dg + 1) * 128, :],
                        in_=Pt[dg % 2][:]).then_inc(RPS.cs[dg % 2].h, 16)
            e.dma_start(out=y0_scr[dg * 128:(dg + 1) * 128, :],
                        in_=y0t[dg % 2][:]).then_inc(RYS.cs[dg % 2].h, 16)
    block.gpsimd(f_store)

    vv_f_end = VV.n
    v_dtot = VV.bump(1)
    s_dtot = SS.bump(1)

    def f2_vector(e):
        e.wait_ge(sem_v, vv_f_end)
        prep3 = dtotT[:].rearrange("p (g n) -> p g n", n=NST)
        for n in range(NST):
            e.tensor_copy(prep3[:, :, n], ptotc[:, :])
        vwait(e)
        e.tensor_mul(dtotT[:], dtotT[:], acol[:])
        vinc(e, 1)
    block.vector(f2_vector)

    def f2_scalar(e):
        e.wait_ge(sem_v, v_dtot)
        e.activation(dtotT[:], dtotT[:], AF.Exp)
        sinc(e, 1)
    block.scalar(f2_scalar)

    barrier()

    zg = scD2.enter_context(nc.sbuf_tensor("zg", [128, TLOC], F32))
    dgt = scD2.enter_context(nc.sbuf_tensor("dgt", [128, TLOC], F32))
    spre = scD2.enter_context(nc.sbuf_tensor("spre", [128, TLOC], F32))
    RZG = Ring(8, 1)
    RDG = Ring(9, 1)
    st0, vv0 = ST.n, VV.n
    ST.n += 32
    cc1 = CC.bump(1)
    VV.n += NCORES
    zg_t = [RZG.cs[0].n + 16 * (j + 1) for j in range(NCORES)]
    dg_t = [RDG.cs[0].n + 16 * (j + 1) for j in range(NCORES)]
    RZG.cs[0].n += 16 * NCORES
    RDG.cs[0].n += 16 * NCORES

    def g_gpsimd(e):
        e.dma_start(out=ag_in_t.ap()[0], in_=zall[:]).then_inc(sem_st, 16)
        e.dma_start(out=ag_in_t.ap()[1], in_=dtotT[:]).then_inc(sem_st, 16)
        e.wait_ge(sem_st, st0 + 32)
        e.collective_compute(
            "AllGather", ALU.bypass,
            replica_groups=[list(range(NCORES))],
            ins=[ag_in_t.ap()],
            outs=[ag_out_t.ap()],
        ).then_inc(sem_cc, 1)
    block.gpsimd(g_gpsimd)

    def g_sync(e):
        e.wait_ge(sem_cc, cc1)
        for j in range(NCORES):
            if j >= 1:
                e.wait_ge(sem_v, vv0 + j)
            e.dma_start(out=zg[:], in_=ag_out_t.ap()[j, 0]
                        ).then_inc(RZG.cs[0].h, 16)
            e.dma_start(out=dgt[:], in_=ag_out_t.ap()[j, 1]
                        ).then_inc(RDG.cs[0].h, 16)
    block.sync(g_sync)

    def g_vector(e):
        e.memset(spre[:], 0.0)
        e.memset(sinT[:], 0.0)
        for j in range(NCORES):
            e.wait_ge(RZG.cs[0].h, zg_t[j])
            e.wait_ge(RDG.cs[0].h, dg_t[j])
            e.tensor_scalar_mul(tmpt[:], spre[:], oh[:, j:j + 1])
            e.tensor_add(sinT[:], sinT[:], tmpt[:])
            e.tensor_mul(spre[:], spre[:], dgt[:])
            e.tensor_add(spre[:], spre[:], zg[:])
            vinc(e, 1)
    block.vector(g_vector)

    barrier()

    Et = [scD2.enter_context(nc.sbuf_tensor(f"hE{i}", [128, TLOC], F32))
          for i in range(2)]
    ECt = scD2.enter_context(nc.sbuf_tensor("hEC", [128, TLOC], F32))
    gat = [scD2.enter_context(nc.sbuf_tensor(f"hg{i}", [128, TLOC], F32))
           for i in range(2)]
    sgt2 = scD2.enter_context(nc.sbuf_tensor("hsg", [128, TLOC], F32))
    yft = [scD2.enter_context(nc.sbuf_tensor(f"hyf{i}", [128, TLOC], BF16))
           for i in range(2)]

    RHY = Ring(0, 2)
    RHP = Ring(2, 2)
    RHU = Ring(4, 2)
    RHG = Ring(6, 2)
    RYF = Ring(8, 2)
    vv0, ss0 = VV.n, SS.n
    SS.n += 17 * 64
    VV.n += 18 * 64
    hy_t, hp_t, hu_t, hg_t, yf_t = [], [], [], [], []
    _r = [[rr.cs[i].n for i in range(2)] for rr in (RHY, RHP, RHU, RHG, RYF)]
    for dg in range(64):
        p = dg % 2
        for ri, tl in enumerate((hy_t, hp_t, hu_t, hg_t, yf_t)):
            _r[ri][p] += 16
            tl.append(_r[ri][p])
    for ri, rr in enumerate((RHY, RHP, RHU, RHG, RYF)):
        for i in range(2):
            rr.cs[i].n = _r[ri][i]

    def h_sync(e):
        for dg in range(64):
            if dg >= 2:
                e.wait_ge(sem_v, vv0 + 18 * (dg - 1))
            e.dma_start(out=y0t[dg % 2][:],
                        in_=y0_scr[dg * 128:(dg + 1) * 128, :]
                        ).then_inc(RHY.cs[dg % 2].h, 16)
            e.dma_start(out=Pt[dg % 2][:],
                        in_=p_scr[dg * 128:(dg + 1) * 128, :]
                        ).then_inc(RHP.cs[dg % 2].h, 16)
            e.dma_start(out=ut[dg % 2][:],
                        in_=u_scr[dg * 128:(dg + 1) * 128, :]
                        ).then_inc(RHU.cs[dg % 2].h, 16)
            e.dma_start(out=gat[dg % 2][:],
                        in_=gate_scr[dg * 128:(dg + 1) * 128, :]
                        ).then_inc(RHG.cs[dg % 2].h, 16)
    block.sync(h_sync)

    def h_scalar(e):
        for dg in range(64):
            e.wait_ge(RHP.cs[dg % 2].h, hp_t[dg])
            e.wait_ge(RHG.cs[dg % 2].h, hg_t[dg])
            for n in range(NST):
                idx = dg * NST + n
                if idx >= 2:
                    dgp, np_ = divmod(idx - 2, NST)
                    e.wait_ge(sem_v, vv0 + 18 * dgp + np_ + 1)
                e.activation(Et[n % 2][:], Pt[dg % 2][:], AF.Exp,
                             scale=acol[:, idx:idx + 1])
                sinc(e, 1)
            if dg >= 1:
                e.wait_ge(sem_v, vv0 + 18 * dg)
            e.activation(sgt2[:], gat[dg % 2][:], AF.Silu)
            sinc(e, 1)
    block.scalar(h_scalar)

    def h_vector(e):
        for dg in range(64):
            e.wait_ge(RHY.cs[dg % 2].h, hy_t[dg])
            e.wait_ge(RHU.cs[dg % 2].h, hu_t[dg])
            if dg >= 2:
                RYF.wait(e, dg % 2, yf_t[dg - 2])
            for n in range(NST):
                e.wait_ge(sem_s, ss0 + 17 * dg + n + 1)
                col = dg * NST + n
                e.tensor_mul(ECt[:], Et[n % 2][:], ctall[:, n, :])
                e.scalar_tensor_tensor(y0t[dg % 2][:], ECt[:],
                                       sinT[:, col:col + 1], y0t[dg % 2][:],
                                       ALU.mult, ALU.add)
                vinc(e, 1)
            e.scalar_tensor_tensor(y0t[dg % 2][:], ut[dg % 2][:],
                                   dcolv[:, dg:dg + 1], y0t[dg % 2][:],
                                   ALU.mult, ALU.add)
            vinc(e, 1)
            e.wait_ge(sem_s, ss0 + 17 * dg + 17)
            e.tensor_mul(yft[dg % 2][:], y0t[dg % 2][:], sgt2[:]
                         )
            vinc(e, 1)
    block.vector(h_vector)

    def h_store(e):
        for dg in range(64):
            e.wait_ge(sem_v, vv0 + 18 * (dg + 1))
            e.dma_start(out=yf_scr[dg * 128:(dg + 1) * 128, :],
                        in_=yft[dg % 2][:]).then_inc(RYF.cs[dg % 2].h, 16)
    block.gpsimd(h_store)

    barrier()
    scD2.close()
    scD.close()

    scI = ExitStack()
    ystat = [scI.enter_context(nc.sbuf_tensor(f"iy{i}", [128, TLOC], BF16))
             for i in range(2)]
    wmov2 = [scI.enter_context(nc.sbuf_tensor(f"iw{i}", [128, 32, 512], BF16))
             for i in range(2)]
    u1res = [scI.enter_context(nc.sbuf_tensor(f"iu{i}", [128, 512], F32))
             for i in range(8)]
    u2t = [scI.enter_context(nc.sbuf_tensor(f"io{i}", [128, 512], F32))
           for i in range(8)]
    pso = [scI.enter_context(nc.psum_tensor(f"ip{i}", [128, 512], F32))
           for i in range(8)]

    RWO = Ring(0, 2)
    RYT = Ring(2, 2)
    RU1 = Ring(4, 8)
    vv0, tt0, st0 = VV.n, TT.n, ST.n
    TT.n += 8 * 64
    VV.n += 8 * 8
    ST.n += 8 * 16 * 8
    wo_t, yt_t = [], []
    _a = [RWO.cs[i].n for i in range(2)]
    _b = [RYT.cs[i].n for i in range(2)]
    for og in range(8):
        _a[0] += 16
        _a[1] += 16
        wo_t.append((_a[0], _a[1]))
        rb = []
        for ic in range(64):
            _b[ic % 2] += 16
            rb.append(_b[ic % 2])
        yt_t.append(rb)
    for i in range(2):
        RWO.cs[i].n = _a[i]
        RYT.cs[i].n = _b[i]
    u1_t = []
    _c = [RU1.cs[i].n for i in range(8)]
    for og in range(8):
        row = []
        for t in range(8):
            _c[t] += 16
            row.append(_c[t])
        u1_t.append(row)
    for i in range(8):
        RU1.cs[i].n = _c[i]

    def i_sync(e):
        for og in range(8):
            if og >= 1:
                e.wait_ge(sem_t, tt0 + 64 * og)
            e.dma_start(out=wmov2[0][:],
                        in_=wout_bf[og, :, 0:32, :]).then_inc(RWO.cs[0].h, 16)
            e.dma_start(out=wmov2[1][:],
                        in_=wout_bf[og, :, 32:64, :]).then_inc(RWO.cs[1].h, 16)
            for ic in range(64):
                flat = 64 * og + ic
                if flat >= 2:
                    e.wait_ge(sem_t, tt0 + flat - 1)
                e.dma_start(out=ystat[ic % 2][:],
                            in_=yf_scr[ic * 128:(ic + 1) * 128, :]
                            ).then_inc(RYT.cs[ic % 2].h, 16)
            if og >= 1:
                e.wait_ge(sem_v, vv0 + 8 * og)
            for t in range(8):
                e.dma_start(out=u1res[t][:],
                            in_=u1_scr[HALO + t * 128:HALO + (t + 1) * 128,
                                       og * 512:(og + 1) * 512]
                            ).then_inc(RU1.cs[t].h, 16)
    block.sync(i_sync)

    def i_tensor(e):
        for og in range(8):
            if og >= 1:
                e.wait_ge(sem_v, vv0 + 8 * og)
            for ic in range(64):
                if ic == 0:
                    e.wait_ge(RWO.cs[0].h, wo_t[og][0])
                if ic == 32:
                    e.wait_ge(RWO.cs[1].h, wo_t[og][1])
                e.wait_ge(RYT.cs[ic % 2].h, yt_t[og][ic])
                ins = None
                for t in range(8):
                    ins = e.matmul(pso[t][:, :],
                                   ystat[ic % 2][:, t * 128:(t + 1) * 128],
                                   wmov2[ic // 32][:, ic % 32, :],
                                   start=(ic == 0), stop=(ic == 63))
                ins.then_inc(sem_t, 1)
    block.tensor(i_tensor)

    def i_vector(e):
        for og in range(8):
            for t in range(8):
                e.wait_ge(sem_t, tt0 + 64 * (og + 1))
                e.wait_ge(RU1.cs[t].h, u1_t[og][t])
                if og >= 1:
                    e.wait_ge(sem_st, st0 + 16 * 8 * og)
                e.tensor_add(u2t[t][:], pso[t][:, :], u1res[t][:]
                             )
                vinc(e, 1)
    block.vector(i_vector)

    def i_store(e):
        for og in range(8):
            for t in range(8):
                e.wait_ge(sem_v, vv0 + 8 * og + t + 1)
                e.dma_start(out=u2_scr[t * 128:(t + 1) * 128,
                                       og * 512:(og + 1) * 512],
                            in_=u2t[t][:]).then_inc(sem_st, 16)
    block.gpsimd(i_store)

    barrier()
    scI.close()

    scJ = ExitStack()
    actT2 = scJ.enter_context(nc.sbuf_tensor("actT2", [128, 32, TLOC], BF16))
    emit_ln_quant(u2_scr, TCH_R, True, actT2, "q3")
    emit_ffn(u2_scr, out_e, TCH_R, actT2, "f2")
    scJ.close()

    if debug:
        dbgs = [("dbg_u1", u1_scr, [TF, H], F32),
                ("dbg_u", u_scr, [DIN, TLOC], F32),
                ("dbg_gate", gate_scr, [DIN, TLOC], F32),
                ("dbg_dt", dt_scr, [DIN, TLOC], F32),
                ("dbg_p", p_scr, [DIN, TLOC], F32),
                ("dbg_y0", y0_scr, [DIN, TLOC], F32),
                ("dbg_yf", yf_scr, [DIN, TLOC], BF16),
                ("dbg_u2", u2_scr, [TLOC, H], F32),
                ("dbg_wdq", wdq_bf, [H, H], BF16),
                ("dbg_win", win_bf, [H, 2 * DIN], BF16)]
        dbg_aps = [(nc.dram_tensor(nm, sh, dt_, kind="ExternalOutput").ap(), srcd)
                   for nm, srcd, sh, dt_ in dbgs]

        def dbg_gp(e):
            for dst, srcd in dbg_aps:
                e.dma_start(out=dst, in_=srcd).then_inc(sem_st, 16)
        ST.n += 16 * len(dbg_aps)
        block.gpsimd(dbg_gp)

    fin_st = ST.n

    def fin_gpsimd(e):
        e.wait_ge(sem_st, fin_st)
    block.gpsimd(fin_gpsimd)

    blk_cm.__exit__(None, None, None)
    pers.close()
    return nc


_NC_CACHE = {}


def kernel(L0_attn_hidden_states, L0_attn_W_in, L0_attn_conv_w, L0_attn_conv_b,
           L0_attn_W_x, L0_attn_W_dt, L0_attn_b_dt, L0_attn_A_log, L0_attn_D,
           L0_attn_W_out, L0_ffn_W_int, L0_ffn_scale_i32, L0_ffn_bias):
    x = np.ascontiguousarray(np.asarray(L0_attn_hidden_states)[0])
    w_in_t = np.ascontiguousarray(np.asarray(L0_attn_W_in).T)
    w_out_t = np.ascontiguousarray(np.asarray(L0_attn_W_out).T)
    w_int_t = np.ascontiguousarray(np.asarray(L0_ffn_W_int).T)
    scale_rep = np.ascontiguousarray(
        np.repeat(np.asarray(L0_ffn_scale_i32).T, 64, axis=0))
    w_x_t = np.ascontiguousarray(np.asarray(L0_attn_W_x).T)
    w_dt_t = np.ascontiguousarray(np.asarray(L0_attn_W_dt).T)
    identf = np.eye(128, dtype=np.float32)

    if "nc" not in _NC_CACHE:
        _NC_CACHE["nc"] = build_nc()
    nc = _NC_CACHE["nc"]

    shared = {
        "w_in_t": w_in_t, "w_out_t": w_out_t, "w_int_t": w_int_t,
        "scale_rep": scale_rep, "w_x_t": w_x_t, "w_dt_t": w_dt_t,
        "convw": np.ascontiguousarray(
            np.asarray(L0_attn_conv_w).reshape(64, 128, 4).transpose(1, 0, 2)),
        "convb": np.ascontiguousarray(
            np.asarray(L0_attn_conv_b).reshape(64, 128).T),
        "bdt": np.ascontiguousarray(np.asarray(L0_attn_b_dt).reshape(64, 128).T),
        "alog": np.ascontiguousarray(
            np.asarray(L0_attn_A_log).reshape(64, 128, NST).transpose(1, 0, 2)
            .reshape(128, 64 * NST)),
        "dvec": np.ascontiguousarray(np.asarray(L0_attn_D).reshape(64, 128).T),
        "fbias": np.ascontiguousarray(np.asarray(L0_ffn_bias).reshape(1, H)),
        "identf": identf,
    }
    in_maps = []
    for c in range(NCORES):
        t0 = c * TLOC
        if c == 0:
            xs = np.concatenate([np.zeros((HALO, H), np.float32), x[0:TLOC]], 0)
        else:
            xs = x[t0 - HALO:t0 + TLOC]
        m = dict(shared)
        m["x"] = np.ascontiguousarray(xs)
        m["mask"] = np.full((128, 1), 0.0 if c == 0 else 1.0, np.float32)
        onehot = np.zeros((128, NCORES), np.float32)
        onehot[:, c] = 1.0
        m["onehot"] = onehot
        in_maps.append(m)

    res = run_bass_kernel_spmd(nc, in_maps, core_ids=list(range(NCORES)))
    _NC_CACHE["last_results"] = res.results
    out = np.concatenate([r["out"] for r in res.results], axis=0)
    return out.reshape(1, 8192, H)
